# version 09
# baseline (speedup 1.0000x reference)
import numpy as np
import ml_dtypes
from contextlib import ExitStack

import math
import concourse.bass as bass
import concourse.mybir as mybir
from concourse.bass_utils import run_bass_kernel_spmd

F32 = mybir.dt.float32
BF16 = mybir.dt.bfloat16
I32 = mybir.dt.int32
AF = mybir.ActivationFunctionType
ALU = mybir.AluOpType
AX = mybir.AxisListType
BFNP = ml_dtypes.bfloat16

D = 1024
INNER = 4096
LNB_DKS = math.log(1.0 / math.sqrt(128.0))
CONSTVALS = [1e-6, 1.0, -0.5, -1.0, LNB_DKS, 1e-5, 0.5, 0.0]
H, DK, DV = 8, 128, 256
NEG = -1.0e30

ENGINES = ("sync", "tensor", "vector", "scalar", "gpsimd")
ESEM = {"sync": "s_sy", "tensor": "s_t", "vector": "s_v", "scalar": "s_sc",
        "gpsimd": "s_g"}
NDMALANES = 12


class Tape:

    def __init__(self, nc):
        self.nc = nc
        self.ops = {e: [] for e in ENGINES}
        self.sems = {}
        self.cnt = {}
        self.writer = {}
        self.readers = {}
        self.waited = {e: {} for e in ENGINES}
        self.lane_next = 0
        self.lane_cnt = [0] * NDMALANES

    def bump(self, sem, n=1):
        self.cnt[sem] = self.cnt.get(sem, 0) + n
        return self.cnt[sem]

    def val(self, sem):
        return self.cnt.get(sem, 0)

    def wait(self, eng, sem, val):
        if val is None or val <= 0:
            return
        if self.waited[eng].get(sem, 0) >= val:
            return
        self.waited[eng][sem] = val
        self.ops[eng].append(lambda e, s=sem, v=val: e.wait_ge(self.sems[s], v))

    def _edges(self, eng, reads, writes):
        own = ESEM[eng]
        for k in reads:
            w = self.writer.get(k)
            if w is not None:
                self.wait(eng, w[0], w[1])
        for k in writes:
            for s, v in self.readers.get(k, {}).items():
                if s == own:
                    continue
                self.wait(eng, s, v)
            w = self.writer.get(k)
            if w is not None and w[0] != own:
                self.wait(eng, w[0], w[1])

    def _record(self, point, reads, writes):
        for k in writes:
            self.writer[k] = point
            self.readers[k] = {}
        for k in reads:
            self.readers.setdefault(k, {})
            self.readers[k][point[0]] = max(self.readers[k].get(point[0], 0),
                                            point[1])

    def group(self, eng, emits, reads=(), writes=(), dma=False, big=True):
        if not isinstance(emits, (list, tuple)):
            emits = [emits]
        pbs = tuple(k for k in reads if k.startswith("pb"))
        if pbs:
            writes = tuple(writes) + pbs
        self._edges(eng, reads, writes)
        if dma and eng != "gpsimd":
            assert len(emits) == 1
            lane = self.lane_next
            self.lane_next = (lane + 1) % NDMALANES
            sem = f"s_d{lane}"
            self.wait(eng, sem, 16 * self.lane_cnt[lane])
            self.lane_cnt[lane] += 1
            val = 16 * self.lane_cnt[lane]
            self.cnt[sem] = val
            inc = 16
        else:
            sem = "s_gd" if dma else ESEM[eng]
            inc = 16 if dma else 1
            val = self.bump(sem, inc)
        for f in emits[:-1]:
            self.ops[eng].append(f)
        self.ops[eng].append(
            lambda e, f=emits[-1], s=sem, i=inc: f(e).then_inc(self.sems[s], i))
        self._record((sem, val), reads, writes)
        return (sem, val)

    def op(self, eng, emit, reads=(), writes=(), dma=False, big=True):
        return self.group(eng, [emit], reads, writes, dma=dma, big=big)

    def barrier(self):
        snap = dict(self.cnt)
        for eng in ENGINES:
            for s, v in snap.items():
                if s == ESEM[eng]:
                    continue
                self.wait(eng, s, v)


def _need(d, k):
    v = d.get(k)
    assert v is not None, k
    return v


def build_core(T=4096, do_ffn1=True, do_attn=True, do_ffn2=True,
               attn_nblocks=None, attn_stage=2, dbg=False):
    NT = T // 128
    nc = bass.Bass("TRN2", target_bir_lowering=False,
                   detect_race_conditions=False,
                   dynamic_dma_scratch_size=8192)
    tp = Tape(nc)
    dt = nc.dram_tensor

    x_d = dt("x", [T, D], F32, kind="ExternalInput").ap()
    out_d = dt("out", [T, D], F32, kind="ExternalOutput").ap()
    wg_n = dt("wg_n", [INNER, D], F32, kind="ExternalInput").ap()
    wu_n = dt("wu_n", [INNER, D], F32, kind="ExternalInput").ap()
    wd_n = dt("wd_n", [D, INNER], F32, kind="ExternalInput").ap()
    wg_t = dt("wg_t", [D, INNER], F32, kind="ExternalInput").ap()
    wu_t = dt("wu_t", [D, INNER], F32, kind="ExternalInput").ap()
    wd_t = dt("wd_t", [INNER, D], F32, kind="ExternalInput").ap()
    wq_d = dt("wq", [D, D], F32, kind="ExternalInput").ap()
    wk_d = dt("wk", [D, D], F32, kind="ExternalInput").ap()
    wv_d = dt("wv", [D, 2 * D], F32, kind="ExternalInput").ap()
    wgv_d = dt("wgv", [D, 2 * D], F32, kind="ExternalInput").ap()
    wo_d = dt("wo", [2 * D, D], F32, kind="ExternalInput").ap()
    wba_d = dt("wba", [D, 16], F32, kind="ExternalInput").ap()
    convq_d = dt("convq", [D, 4], F32, kind="ExternalInput").ap()
    convk_d = dt("convk", [D, 4], F32, kind="ExternalInput").ap()
    convv_d = dt("convv", [2 * D, 4], F32, kind="ExternalInput").ap()
    alog_d = dt("alog", [8, 1], F32, kind="ExternalInput").ap()
    dtb_d = dt("dtb", [8, 1], F32, kind="ExternalInput").ap()
    normw_d = dt("normw", [128, 2], F32, kind="ExternalInput").ap()
    cvals_d = dt("cvals", [128, 8], F32, kind="ExternalInput").ap()
    identb_d = dt("identb", [128, 128], BF16, kind="ExternalInput").ap()
    onesf_d = dt("onesf", [1, 128], F32, kind="ExternalInput").ap()
    monesf_d = dt("monesf", [1, 128], F32, kind="ExternalInput").ap()
    mask_ksl_d = dt("mask_ksl", [128, 128], F32, kind="ExternalInput").ap()
    onesf128_d = dt("onesf128", [128, 128], F32, kind="ExternalInput").ap()
    identf_d = dt("identf", [128, 128], F32, kind="ExternalInput").ap()

    tern_g_d = dt("tern_g", [D, INNER], BF16, kind="Internal").ap()
    tern_u_d = dt("tern_u", [D, INNER], BF16, kind="Internal").ap()
    tern_d_d = dt("tern_d", [INNER, D], BF16, kind="Internal").ap()
    res_d = dt("res_dram", [T, D], F32, kind="Internal").ap()
    hq_d = dt("hq_dram", [T, INNER], BF16, kind="Internal").ap()

    es = ExitStack()
    SB = lambda n, s, d: es.enter_context(nc.sbuf_tensor(n, s, d))
    PS = lambda n, s, d: es.enter_context(nc.psum_tensor(n, s, d))

    with es:
        A = SB("arenaA", [128, 65536], BF16)
        B = SB("arenaB", [128, 20480], BF16)
        B32 = B[:].bitcast(F32)

        rt0 = SB("rt0", [128, 1024], F32)
        rt1 = SB("rt1", [128, 1024], F32)
        xln = SB("xln", [128, 1024], F32)
        zk = SB("zk", [128, 1024], F32)
        qi = SB("qi", [128, 512], I32)
        S1 = SB("S1", [128, 4096], BF16)
        S132 = S1[:].bitcast(F32)
        xqb0 = S1[:, 0:1024]
        xqb1 = S1[:, 1024:2048]
        xqf0 = S1[:, 2048:3072]
        xqf1 = S1[:, 3072:4096]
        S2 = SB("S2", [128, 4096], BF16)
        S232 = S2[:].bitcast(F32)
        colA = SB("colA", [128, 16], F32)
        colB = SB("colB", [128, 64], F32)
        hsc = SB("hsc", [128, 32], F32)
        wst = SB("wst", [128, 32], F32)
        stt_ = SB("stt_", [128, 8], F32)
        sc_c = SB("sc_c", [1, 32], F32)
        identb = SB("identb_s", [128, 128], BF16)
        onesf = SB("onesf_s", [1, 128], F32)
        monesf = SB("monesf_s", [1, 128], F32)
        mask_ksl = SB("mask_ksl_s", [128, 128], F32)
        onesf128 = SB("onesf128_s", [128, 128], F32)
        identf = SB("identf_s", [128, 128], F32)
        normw_c = SB("normw_c", [128, 2], F32)
        cvals = SB("cvals_s", [128, 8], F32)
        for _j, _v in enumerate(CONSTVALS):
            nc.const_aps.register(F32, float(_v), cvals[:, _j:_j + 1])
        convw = SB("convw", [128, 128], F32)
        wba_sb = SB("wba_sb", [128, 128], BF16)
        wba_f = SB("wba_f", [128, 128], F32)
        xtr = SB("xtr", [128, 264], F32)
        alog_c = SB("alog_c", [8, 4], F32)
        gch = SB("gch", [16, 128], F32)
        bch = SB("bch", [16, 128], F32)
        bech = SB("bech", [16, 128], F32)
        ehat = SB("ehat", [16, 128], F32)
        bT = SB("bT", [128, 16], F32)
        bnT = SB("bnT", [128, 16], F32)
        betaT = SB("betaT", [128, 16], F32)
        nbetaT = SB("nbetaT", [128, 16], F32)
        gamT = SB("gamT", [128, 16], F32)
        bgT = SB("bgT", [128, 16], F32)
        ehatT2 = SB("ehatT2", [128, 16], F32)
        ebC = SB("ebC", [128, 16], F32)
        convtail = SB("convtail", [128, 96], F32)

        pb = [PS(f"pb{i}", [128, 512], F32) for i in range(8)]

        for s in sorted(set(ESEM.values()) | {"s_gd"}
                        | {f"s_d{i}" for i in range(NDMALANES)}):
            tp.sems[s] = es.enter_context(nc.semaphore(s))

        def _sbparts(ap):
            try:
                if type(ap.tensor).__name__ == "SBTensorHandle":
                    return int(ap.shape[0])
            except Exception:
                pass
            return None

        def ldma(dst, src, reads=(), writes=(), eng="sync", drain=False):
            return tp.op(eng, lambda e, d=dst, s=src: e.dma_start(out=d, in_=s),
                         reads=reads, writes=writes, dma=True)

        def T_(emits, reads=(), writes=()):
            return tp.group("tensor", emits, reads, writes)

        def V_(emits, reads=(), writes=()):
            return tp.group("vector", emits, reads, writes)

        def SC_(emits, reads=(), writes=()):
            return tp.group("scalar", emits, reads, writes)

        def G_(emits, reads=(), writes=()):
            return tp.group("gpsimd", emits, reads, writes)

        def MM(out_ap, lhsT_ap, rhs_ap, start, stop):
            return (lambda e, o=out_ap, l=lhsT_ap, r=rhs_ap, a=start, b=stop:
                    e.matmul(out=o, lhsT=l, rhs=r, start=a, stop=b))

        ldma(identb[:], identb_d, writes=("identb",))
        ldma(onesf[:], onesf_d, writes=("onesf",))
        ldma(monesf[:], monesf_d, writes=("monesf",))
        ldma(mask_ksl[:], mask_ksl_d, writes=("mask_ksl",))
        ldma(onesf128[:], onesf128_d, writes=("onesf128",))
        ldma(identf[:], identf_d, writes=("identf",))
        ldma(normw_c[:], normw_d, writes=("normw",))
        ldma(cvals[:], cvals_d, writes=("cvals",))
        ldma(alog_c[0:8, 0:1], alog_d, writes=("alog",))
        ldma(alog_c[0:8, 2:3], dtb_d, writes=("alog2",))

        def weight_stats(w_nat, big_rows, tag, c_slot):
            w1_dram = dt(f"w1_{tag}", [128, 32] if big_rows else [32, 128],
                         F32, kind="Internal").ap()
            w2_dram = dt(f"w2_{tag}", [4, 1], F32, kind="Internal").ap()
            if big_rows:
                src = w_nat.rearrange("(R r) (G i c) -> R G i r c",
                                      R=128, r=32, G=8, i=4, c=32)
            else:
                src = None
            for ps in range(8):
                bufk = "Ba" if ps % 2 == 0 else "Bb"
                buf = B32[:, (ps % 2) * 4096:(ps % 2) * 4096 + 4096]
                for iw in range(4):
                    if big_rows:
                        s_ap = bass.AP(w_nat.tensor,
                                       (ps * 4 + iw) * 32,
                                       [[32768, 128], [1024, 32], [1, 32]])
                        ldma(buf[:, iw * 1024:(iw + 1) * 1024]
                             .rearrange("p (r c) -> p r c", r=32, c=32),
                             s_ap, writes=(bufk,), drain=True)
                    else:
                        for gg in range(4):
                            s_ap = bass.AP(
                                w_nat.tensor,
                                gg * 1024 + (ps * 4 + iw) * 32,
                                [[131072, 32], [4096, 32], [1, 32]])
                            ldma(buf[gg * 32:(gg + 1) * 32,
                                     iw * 1024:(iw + 1) * 1024]
                                 .rearrange("p (r c) -> p r c", r=32, c=32),
                                 s_ap, writes=(bufk,))
                V_(lambda e, b=buf, ps=ps: e.tensor_reduce(
                    out=wst[:, ps * 4:(ps + 1) * 4]
                        .rearrange("p (a u) -> p a u", a=4, u=1),
                    in_=b.rearrange("p (a b c) -> p a b c", a=4, b=32, c=32),
                    axis=AX.XY, op=ALU.add, apply_absolute_value=True),
                   reads=(bufk,), writes=(f"wst{ps}",))
            if big_rows:
                ldma(w1_dram, wst[:, 0:32],
                     reads=tuple(f"wst{j}" for j in range(8)), writes=("w1",))
                ldma(B32[0:4, 0:1024],
                     w1_dram.rearrange("(a r) c -> a (r c)", a=4, r=32),
                     reads=("w1",), writes=("stg2", "Ba"))
            else:
                ldma(bass.AP(w1_dram.tensor, 0,
                             [[32, 4], [128, 32], [1, 32]]),
                     wst[:, 0:32],
                     reads=tuple(f"wst{j}" for j in range(8)), writes=("w1",))
                ldma(B32[0:4, 0:1024],
                     bass.AP(w1_dram.tensor, 0,
                             [[32, 4], [128, 32], [1, 32]]),
                     reads=("w1",), writes=("stg2", "Ba"))
            V_(lambda e: e.tensor_reduce(out=stt_[0:4, 0:1],
                                         in_=B32[0:4, 0:1024],
                                         axis=AX.X, op=ALU.add),
               reads=("stg2",), writes=("stg3",))
            ldma(w2_dram, stt_[0:4, 0:1], reads=("stg3",), writes=("w2",))
            ldma(sc_c[0:1, 8:12],
                 w2_dram.rearrange("(u a) c -> u (a c)", u=1, a=4),
                 reads=("w2",), writes=("stg4",))
            V_(lambda e: e.tensor_reduce(out=sc_c[0:1, 12:13],
                                         in_=sc_c[0:1, 8:12],
                                         axis=AX.X, op=ALU.add),
               reads=("stg4",), writes=("stg5",))
            V_(lambda e, cs=c_slot: e.tensor_scalar(
                out=sc_c[0:1, cs:cs + 1], in0=sc_c[0:1, 12:13],
                scalar1=float(1.0 / 4194304.0), scalar2=1e-5,
                op0=ALU.mult, op1=ALU.max),
               reads=("stg5",), writes=(f"c_{tag}",))
            V_(lambda e, cs=c_slot: e.reciprocal(
                out=sc_c[0:1, cs + 1:cs + 2], in_=sc_c[0:1, cs:cs + 1]),
               reads=(f"c_{tag}",), writes=(f"s_{tag}",))

        weight_stats(wg_n, True, "g", 0)
        weight_stats(wu_n, True, "u", 2)
        weight_stats(wd_n, False, "d", 4)

        def bcast_col(src_ap, dst_col, rkey, wkey):
            T_([MM(pb[7][:, 0:1], onesf[0:1, :], src_ap, True, True)],
               reads=("onesf", rkey), writes=("pb7",))
            V_(lambda e, d=dst_col: e.tensor_copy(out=d, in_=pb[7][:, 0:1]),
               reads=("pb7",), writes=(wkey,))

        bcast_col(sc_c[0:1, 1:2], colA[:, 0:1], "s_g", "colA_sg")
        bcast_col(sc_c[0:1, 3:4], colA[:, 1:2], "s_u", "colA_su")
        bcast_col(sc_c[0:1, 5:6], colA[:, 2:3], "s_d", "colA_sd")
        bcast_col(sc_c[0:1, 0:1], colA[:, 3:4], "c_g", "colA_cg")
        bcast_col(sc_c[0:1, 2:3], colA[:, 4:5], "c_u", "colA_cu")
        bcast_col(sc_c[0:1, 4:5], colA[:, 5:6], "c_d", "colA_cd")

        def ternarize(w_t_fp32, scale_col, scol_key, dst_sb, dst_key,
                      dst_dram, nrt, ncol):
            for rt in range(nrt):
                bufk = "Ba" if rt % 2 == 0 else "Bb"
                buf = B32[:, (rt % 2) * 4096:(rt % 2) * 4096 + ncol]
                cmp_ = B32[:, 8192:10240]
                ldma(buf, w_t_fp32[rt * 128:(rt + 1) * 128, :], writes=(bufk,))
                V_(lambda e, b=buf, sc=scale_col: e.tensor_scalar(
                    out=b, in0=b, scalar1=sc, scalar2=None, op0=ALU.mult),
                   reads=(bufk, scol_key), writes=(bufk,))
                nh = max(1, ncol // 2048)
                hw = min(2048, ncol)
                for hh in range(nh):
                    bh = buf[:, hh * hw:(hh + 1) * hw]
                    cm = cmp_[:, 0:hw]
                    V_(lambda e, b=bh, c=cm: e.tensor_scalar(
                        out=c, in0=b, scalar1=0.5, scalar2=None,
                        op0=ALU.is_gt),
                       reads=(bufk,), writes=("Bc",))
                    V_(lambda e, b=bh, c=cm: e.scalar_tensor_tensor(
                        out=c, in0=b, scalar=-0.5, in1=c,
                        op0=ALU.is_ge, op1=ALU.add),
                       reads=(bufk, "Bc"), writes=("Bc",))
                    if dst_sb is not None:
                        dd = dst_sb[:, rt * ncol + hh * hw:
                                    rt * ncol + (hh + 1) * hw]
                        dk = dst_key + "_t"
                    else:
                        dd = S2[:, hh * hw:(hh + 1) * hw]
                        dk = dst_key + "_s"
                    V_(lambda e, d=dd, c=cm: e.tensor_scalar(
                        out=d, in0=c, scalar1=-1.0, scalar2=None,
                        op0=ALU.add),
                       reads=("Bc",), writes=(dk,))
                if dst_dram is not None:
                    src_sb = (dst_sb[:, rt * ncol:(rt + 1) * ncol]
                              if dst_sb is not None else S2[:, 0:ncol])
                    rk = dst_key + ("_t" if dst_sb is not None else "_s")
                    ldma(dst_dram[rt * 128:(rt + 1) * 128, :], src_sb,
                         reads=(rk,), writes=(dst_key + "_dram",))

        A_gate = A[:, 0:32768]
        A_up = A[:, 32768:65536]
        ternarize(wg_t, colA[:, 0:1], "colA_sg", A_gate, "terng",
                  tern_g_d, 8, 4096)
        ternarize(wu_t, colA[:, 1:2], "colA_su", A_up, "ternu",
                  tern_u_d, 8, 4096)
        ternarize(wd_t, colA[:, 2:3], "colA_sd", None, "ternd",
                  tern_d_d, 32, 1024)

        class Ctx:
            def __init__(self):
                self.l = []
            def T(self, ems, reads=(), writes=()):
                self.l.append(("tensor", ems, reads, writes))
            def V(self, em, reads=(), writes=()):
                self.l.append(("vector", [em], reads, writes))
            def SC(self, em, reads=(), writes=()):
                self.l.append(("scalar", [em], reads, writes))
            def DMA(self, dst, src, reads=(), writes=()):
                self.l.append(("dma",
                               [lambda e, d=dst, s=src:
                                e.dma_start(out=d, in_=s)], reads, writes))

        class Direct:
            def T(self, ems, reads=(), writes=()):
                tp.group("tensor", ems, reads, writes)
            def V(self, em, reads=(), writes=()):
                tp.op("vector", em, reads, writes)
            def SC(self, em, reads=(), writes=()):
                tp.op("scalar", em, reads, writes)
            def DMA(self, dst, src, reads=(), writes=()):
                ldma(dst, src, reads=reads, writes=writes)

        direct = Direct()

        def emit_ctx(lst):
            for eng, ems, rd, wr in lst:
                if eng == "dma":
                    tp.op("sync", ems[0], rd, wr, dma=True)
                else:
                    tp.group(eng, ems, rd, wr)

        def merge2(a, bq):
            out = []
            for i in range(max(len(a), len(bq))):
                if i < len(a):
                    out.append(a[i])
                if i < len(bq):
                    out.append(bq[i])
            return out

        def layer_norm(rt, rtk, par, E=None, xl=None, xlk=None, sqpb=None,
                       first_extra=()):
            if E is None:
                E = direct
            if xl is None:
                xl, xlk = xln[:], "xln"
            cb = par * 32
            c = lambda j: colB[:, cb + j:cb + j + 1]
            ck = lambda j: f"colB{par}_{j}"
            E.V(lambda e: e.reduce_sum(out=c(0), in_=rt[:], axis=AX.X),
                reads=(rtk,), writes=(ck(0),))
            E.V(lambda e: e.tensor_scalar(out=c(1), in0=c(0),
                                          scalar1=float(-1.0 / D),
                                          scalar2=None, op0=ALU.mult),
                reads=(ck(0),), writes=(ck(1),))
            E.V(lambda e, xl=xl: e.tensor_scalar(out=xl, in0=rt[:],
                                                 scalar1=c(1),
                                                 scalar2=None, op0=ALU.add),
                reads=(rtk, ck(1)), writes=(xlk,) + tuple(first_extra))
            if sqpb is None:
                E.SC(lambda e, xl=xl: e.activation(out=zk[:], in_=xl,
                                                   func=AF.Square,
                                                   accum_out=c(2)),
                     reads=(xlk,), writes=("zk_s0", "zk_s1", ck(2)))
            else:
                pbank, pbk = sqpb
                E.SC(lambda e, xl=xl: e.activation(
                    out=pbank[:, 0:512], in_=xl[:, 0:512], func=AF.Square,
                    accum_out=c(24)),
                     reads=(xlk,), writes=(pbk, ck(24)))
                E.SC(lambda e, xl=xl: e.activation(
                    out=pbank[:, 0:512], in_=xl[:, 512:1024], func=AF.Square,
                    accum_out=c(25)),
                     reads=(xlk,), writes=(pbk, ck(25)))
                E.V(lambda e: e.tensor_tensor(out=c(2), in0=c(24), in1=c(25),
                                              op=ALU.add),
                    reads=(ck(24), ck(25)), writes=(ck(2),))
            E.V(lambda e: e.tensor_scalar(out=c(3), in0=c(2),
                                          scalar1=float(1.0 / D), scalar2=1e-5,
                                          op0=ALU.mult, op1=ALU.add),
                reads=(ck(2),), writes=(ck(3),))
            E.SC(lambda e: e.activation(out=c(4), in_=c(3), func=AF.Ln),
                 reads=(ck(3),), writes=(ck(4),))
            E.SC(lambda e: e.activation(out=c(5), in_=c(4), func=AF.Exp,
                                        scale=-0.5),
                 reads=(ck(4),), writes=(ck(5),))
            E.V(lambda e, xl=xl: e.tensor_scalar(out=xl, in0=xl, scalar1=c(5),
                                                 scalar2=None, op0=ALU.mult),
                reads=(xlk, ck(5)), writes=(xlk,))
            return c, ck

        def ln_and_quant(tile, rt, rtk, par, E):
            if par == 0:
                xl, xlk = xln[:], "xln"
                zsl = lambda hh: (zk[:, hh * 512:(hh + 1) * 512],
                                  f"zk_s{hh}")
            else:
                xl, xlk = S232[:, 0:1024], "xln1"
                zsl = lambda hh: (S232[:, 1024 + hh * 512:
                                       1024 + (hh + 1) * 512], f"z1_{hh}")
            c, ck = layer_norm(rt, rtk, par, E=E, xl=xl, xlk=xlk,
                               sqpb=(pb[6 + par], f"pb{6 + par}"),
                               first_extra=(("ternd_s",) if par == 1 else ()))
            E.V(lambda e, xl=xl: e.reduce_max(out=c(6), in_=xl, axis=AX.X,
                                              apply_absolute_value=True),
                reads=(xlk,), writes=(ck(6),))
            E.V(lambda e: e.tensor_scalar(out=c(7), in0=c(6), scalar1=1e-5,
                                          scalar2=None, op0=ALU.max),
                reads=(ck(6),), writes=(ck(7),))
            E.V(lambda e: e.reciprocal(out=c(8), in_=c(7)),
                reads=(ck(7),), writes=(ck(8),))
            E.V(lambda e: e.tensor_scalar(out=c(9), in0=c(8), scalar1=127.0,
                                          scalar2=None, op0=ALU.mult),
                reads=(ck(8),), writes=(ck(9),))
            E.V(lambda e: e.reciprocal(out=c(10), in_=c(9)),
                reads=(ck(9),), writes=(ck(10),))
            E.V(lambda e: e.tensor_tensor(out=c(11), in0=c(10),
                                          in1=colA[:, 3:4], op=ALU.mult),
                reads=(ck(10), "colA_cg"), writes=(ck(11),))
            E.V(lambda e: e.tensor_tensor(out=c(12), in0=c(10),
                                          in1=colA[:, 4:5], op=ALU.mult),
                reads=(ck(10), "colA_cu"), writes=(ck(12),))
            xqb = xqb0 if par == 0 else xqb1
            for hh in range(2):
                zs, zsk = zsl(hh)
                E.V(lambda e, z=zs, xl=xl, hh=hh: e.tensor_scalar(
                    out=z, in0=xl[:, hh * 512:(hh + 1) * 512], scalar1=c(9),
                    scalar2=0.5, op0=ALU.mult, op1=ALU.add),
                    reads=(xlk, ck(9)), writes=(zsk,))
                E.SC(lambda e, z=zs: e.copy(out=qi[:], in_=z),
                     reads=(zsk,), writes=("qi_s",))
                E.V(lambda e, z=zs: e.scalar_tensor_tensor(
                    out=z, in0=qi[:], scalar=1.0, in1=z,
                    op0=ALU.mult, op1=ALU.is_gt),
                    reads=("qi_s", zsk), writes=(zsk,))
                E.V(lambda e, z=zs, d=xqb, hh=hh: e.tensor_tensor(
                    out=d[:, hh * 512:(hh + 1) * 512], in0=qi[:], in1=z,
                    op=ALU.subtract),
                    reads=("qi_s", zsk), writes=(f"xqb{par}",))
            return c, ck

        def transpose_128(src_ap, srck, dst_ap, dstk, nblk, pbank, pbkey,
                          E=None):
            if E is None:
                E = direct
            for g0 in range(0, nblk, 4):
                gn = min(4, nblk - g0)
                ems = []
                for j in range(gn):
                    ems.append(MM(pbank[:, j * 128:(j + 1) * 128],
                                  src_ap[:, (g0 + j) * 128:(g0 + j + 1) * 128],
                                  identb[:], True, True))
                E.T(ems, reads=(srck, "identb"), writes=(pbkey,))
                E.SC(lambda e, g0=g0, gn=gn: e.copy(
                    out=dst_ap[:, g0 * 128:(g0 + gn) * 128],
                    in_=pbank[:, 0:gn * 128]),
                     reads=(pbkey,), writes=(dstk,))

        def sweep1_partA(i, src_dram):
            E = Ctx()
            par = i % 2
            rt = rt0 if par == 0 else rt1
            rtk = f"rt{par}"
            E.DMA(rt[:], src_dram[i * 128:(i + 1) * 128, :],
                  reads=(f"res{i}",), writes=(rtk,))
            c, ck = ln_and_quant(i, rt, rtk, par, E)
            xqb = xqb0 if par == 0 else xqb1
            xqf = xqf0 if par == 0 else xqf1
            transpose_128(xqb[:], f"xqb{par}", xqf[:], f"xqf{par}", 8,
                          pb[4 + par][:, 0:512], f"pb{4 + par}", E=E)
            return E.l, c, ck

        def sweep1_partB(i, c, ck):
            E = Ctx()
            par = i % 2
            xqf = xqf0 if par == 0 else xqf1
            hbase = par * 4096
            eg_ap, eu_ap = c(11), c(12)
            if par == 0:
                zsl = lambda s: (zk[:, (s % 2) * 512:(s % 2) * 512 + 512],
                                 f"zk_s{s % 2}")
            else:
                zsl = lambda s: (S232[:, 1024 + (s % 2) * 512:
                                      1024 + (s % 2) * 512 + 512],
                                 f"z1_{s % 2}")
            for ic in range(8):
                bk = ic % 2
                hsl = B32[:, hbase + ic * 512:hbase + (ic + 1) * 512]
                ems = [MM(pb[bk][:, 0:512],
                          xqf[:, fc * 128:(fc + 1) * 128],
                          A_gate[:, fc * 4096 + ic * 512:
                                 fc * 4096 + (ic + 1) * 512],
                          fc == 0, fc == 7) for fc in range(8)]
                E.T(ems, reads=(f"xqf{par}", "terng_t"), writes=(f"pb{bk}",))
                E.SC(lambda e, h=hsl, b=bk, s=eg_ap: e.activation(
                    out=h, in_=pb[b][:, 0:512], func=AF.Silu, scale=s),
                     reads=(f"pb{bk}", ck(11)),
                     writes=(f"h{par}_{ic}",))
                ems = [MM(pb[2 + bk][:, 0:512],
                          xqf[:, fc * 128:(fc + 1) * 128],
                          A_up[:, fc * 4096 + ic * 512:
                               fc * 4096 + (ic + 1) * 512],
                          fc == 0, fc == 7) for fc in range(8)]
                E.T(ems, reads=(f"xqf{par}", "ternu_t"),
                    writes=(f"pb{2 + bk}",))
                E.V(lambda e, h=hsl, b=bk, s=eu_ap: e.scalar_tensor_tensor(
                    out=h, in0=pb[2 + b][:, 0:512], scalar=s, in1=h,
                    op0=ALU.mult, op1=ALU.mult),
                    reads=(f"pb{2 + bk}", ck(12), f"h{par}_{ic}"),
                    writes=(f"h{par}_{ic}",))
            hkeys = tuple(f"h{par}_{ic}" for ic in range(8))
            hfull = B32[:, hbase:hbase + 4096]
            E.V(lambda e, h=hfull, o=c(16): e.reduce_max(
                out=o, in_=h, axis=AX.X, apply_absolute_value=True),
                reads=hkeys, writes=(ck(16),))
            E.V(lambda e, a=c(16), o=c(17): e.tensor_scalar(
                out=o, in0=a, scalar1=1e-5, scalar2=None, op0=ALU.max),
                reads=(ck(16),), writes=(ck(17),))
            E.V(lambda e, a=c(17), o=c(18): e.reciprocal(out=o, in_=a),
                reads=(ck(17),), writes=(ck(18),))
            E.V(lambda e, a=c(18), o=c(19): e.tensor_scalar(
                out=o, in0=a, scalar1=127.0, scalar2=None, op0=ALU.mult),
                reads=(ck(18),), writes=(ck(19),))
            E.V(lambda e, a=c(19), o=c(20): e.reciprocal(out=o, in_=a),
                reads=(ck(19),), writes=(ck(20),))
            E.V(lambda e, a=c(20), o=c(21): e.tensor_tensor(
                out=o, in0=a, in1=colA[:, 5:6], op=ALU.mult),
                reads=(ck(20), "colA_cd"), writes=(ck(21),))
            E.V(lambda e, a=c(21), i=i: e.tensor_scalar(
                out=hsc[:, i:i + 1], in0=a, scalar1=0.5, scalar2=None,
                op0=ALU.mult),
                reads=(ck(21),), writes=(f"hsc{i}",))
            hq_sb = B[:, 16384:20480]
            hscale_ap = c(19)
            for s in range(8):
                sl = slice(s * 512, (s + 1) * 512)
                hk = f"h{par}_{s}"
                hslice = B32[:, hbase + sl.start:hbase + sl.stop]
                zs, zsk = zsl(s)
                E.V(lambda e, h=hslice, z=zs, sc=hscale_ap: e.tensor_scalar(
                    out=z, in0=h, scalar1=sc, scalar2=0.5,
                    op0=ALU.mult, op1=ALU.add),
                    reads=(hk, ck(19)), writes=(zsk,))
                E.SC(lambda e, z=zs: e.copy(out=qi[:], in_=z),
                     reads=(zsk,), writes=("qi_s",))
                E.V(lambda e, z=zs: e.scalar_tensor_tensor(
                    out=z, in0=qi[:], scalar=1.0, in1=z, op0=ALU.mult,
                    op1=ALU.is_gt),
                    reads=("qi_s", zsk), writes=(zsk,))
                E.V(lambda e, z=zs, sl=sl: e.tensor_tensor(
                    out=hq_sb[:, sl], in0=qi[:], in1=z,
                    op=ALU.subtract),
                    reads=("qi_s", zsk), writes=(f"hq_{s}",))
            E.DMA(hq_d[i * 128:(i + 1) * 128, :], hq_sb,
                  reads=tuple(f"hq_{s}" for s in range(8)),
                  writes=("hq_dram",))
            return E.l

        def ffn_sweep1(src_dram, phase):
            prevB = None
            for i in range(NT):
                la, c, ck = sweep1_partA(i, src_dram)
                if prevB is None:
                    emit_ctx(la)
                else:
                    emit_ctx(merge2(prevB, la))
                prevB = sweep1_partB(i, c, ck)
            emit_ctx(prevB)

        def sweep2_partA(i, res_src):
            E = Ctx()
            par = i % 2
            hqv = B[:, par * 4096:(par + 1) * 4096]
            hqk = f"s2hq{par}"
            hqf = B[:, 8192 + par * 4096:8192 + (par + 1) * 4096]
            hqfk = f"s2hqf{par}"
            E.DMA(hqv, hq_d[i * 128:(i + 1) * 128, :],
                  reads=("hq_dram",), writes=(hqk,))
            transpose_128(hqv, hqk, hqf, hqfk, 32,
                          pb[4 + par][:, 0:512], f"pb{4 + par}", E=E)
            rt = rt0 if par == 0 else rt1
            rtk = f"rt{par}"
            E.DMA(rt[:], res_src[i * 128:(i + 1) * 128, :],
                  reads=(f"res{i}",), writes=(rtk,))
            return E.l

        def sweep2_partB(i, dst_dram):
            E = Ctx()
            par = i % 2
            hqf = B[:, 8192 + par * 4096:8192 + (par + 1) * 4096]
            hqfk = f"s2hqf{par}"
            rt = rt0 if par == 0 else rt1
            rtk = f"rt{par}"
            for h2 in range(2):
                bk = par * 2 + h2
                ems = [MM(pb[bk][:, 0:512],
                          hqf[:, ic * 128:(ic + 1) * 128],
                          A[:, ic * 1024 + h2 * 512:
                            ic * 1024 + h2 * 512 + 512],
                          ic == 0, ic == 31) for ic in range(32)]
                E.T(ems, reads=(hqfk, "Adown"), writes=(f"pb{bk}",))
                E.V(lambda e, h2=h2, bk=bk, i=i, r=rt: e.scalar_tensor_tensor(
                    out=r[:, h2 * 512:(h2 + 1) * 512],
                    in0=pb[bk][:, 0:512], scalar=hsc[:, i:i + 1],
                    in1=r[:, h2 * 512:(h2 + 1) * 512],
                    op0=ALU.mult, op1=ALU.add),
                    reads=(f"pb{bk}", f"hsc{i}", rtk), writes=(rtk,))
            E.DMA(dst_dram[i * 128:(i + 1) * 128, :], rt[:],
                  reads=(rtk,), writes=(f"res{i}",))
            return E.l

        def ffn_sweep2(res_src, dst_dram):
            ldma(A[:, 0:32768],
                 tern_d_d.rearrange("(a p) c -> p a c", a=32, p=128),
                 reads=("ternd_dram",), writes=("Adown",))
            prevB = None
            for i in range(NT):
                la = sweep2_partA(i, res_src)
                if prevB is None:
                    emit_ctx(la)
                else:
                    emit_ctx(merge2(prevB, la))
                prevB = sweep2_partB(i, dst_dram)
            emit_ctx(prevB)

        def build_attn():
            NB = T // 256
            QTf = lambda h, cc: B[:, h * 256 + cc * 128:h * 256 + cc * 128 + 128]
            Kf_ = lambda h, cc: B[:, 2048 + h * 256 + cc * 128:
                                  2048 + h * 256 + cc * 128 + 128]
            BKt = lambda h, cc: B[:, 4096 + (cc * 8 + h) * 128:
                                  4096 + (cc * 8 + h) * 128 + 128]
            Kh_ = lambda h, cc: B[:, 6144 + (cc * 8 + h) * 128:
                                  6144 + (cc * 8 + h) * 128 + 128]
            Vb_ = lambda h, cc: B[:, 8192 + (cc * 8 + h) * 256:
                                  8192 + (cc * 8 + h) * 256 + 256]
            GVs = lambda cc, h: B[:, 12288 + h * 256:12288 + h * 256 + 256]
            xlnfm = lambda fc: B[:, 16384 + fc * 256:16384 + fc * 256 + 256]
            xlnb = B[:, 14336:15360]
            Sbf = lambda h: B[:, 18432 + h * 256:18432 + (h + 1) * 256]
            Sst = lambda h: S132[:, h * 256:(h + 1) * 256]
            G32 = S232[:, 0:128]
            Dls = S232[:, 128:256]
            D2_ = S232[:, 256:384]
            DA_ = S232[:, 384:512]
            Nt_ = S2[:, 1024:1152]
            NTt = S2[:, 1152:1280]
            Aqk = S2[:, 1280:1408]
            Pw = lambda j: S2[:, 1408 + j * 128:1408 + (j + 1) * 128]
            sqd = S232[:, 704:960]
            qsb = S232[:, 960:1216]
            ogt = S2[:, 2432:2560]
            DAb = S2[:, 2560:2688]
            Yc = lambda j: S2[:, 2688 + j * 128:2688 + (j + 1) * 128]
            Vp_ = S232[:, 1472:1728]
            WT_ = S2[:, 3456:3584]
            DAt = S2[:, 3712:3840]
            ogf = S2[:, 3712:3840]
            U_ = S2[:, 3840:4096]
            og2 = S2[:, 3840:4096]

            def wload(dst_off, w_d_, nblk, blkcols):
                for a_ in range(nblk):
                    par = a_ % 2
                    buf = B32[:, par * 2048:par * 2048 + blkcols]
                    ldma(buf, w_d_[a_ * 128:(a_ + 1) * 128, :],
                         writes=(f"wldB{par}",))
                    V_(lambda e, b=buf, a_=a_: e.tensor_copy(
                        out=A[:, dst_off + a_ * blkcols:
                              dst_off + (a_ + 1) * blkcols],
                        in_=b),
                       reads=(f"wldB{par}",),
                       writes=("Wq" if dst_off == 0 else
                               "Wk" if dst_off == 8192 else
                               "Wv" if dst_off == 16384 else
                               "Wgv" if dst_off == 32768 else "Wo",
                               f"wldB{par}"))
            wload(0, wq_d, 8, 1024)
            wload(8192, wk_d, 8, 1024)
            wload(16384, wv_d, 8, 2048)
            wload(32768, wgv_d, 8, 2048)
            wload(49152, wo_d, 16, 1024)
            ldma(wba_f[:], wba_d.rearrange("(a p) c -> p a c", a=8, p=128),
                 writes=("wbaf",))
            V_(lambda e: e.tensor_copy(out=wba_sb[:], in_=wba_f[:]),
               reads=("wbaf",), writes=("wba",))
            ldma(convw[:, 0:32],
                 convq_d.rearrange("(a p) c -> p a c", a=8, p=128),
                 writes=("convw",))
            ldma(convw[:, 32:64],
                 convk_d.rearrange("(a p) c -> p a c", a=8, p=128),
                 writes=("convw2",))
            ldma(convw[:, 64:128],
                 convv_d.rearrange("(a p) c -> p a c", a=16, p=128),
                 writes=("convw3",))
            for ic in range(16):
                V_(lambda e, ic=ic: e.tensor_scalar(
                    out=A[:, 49152 + ic * 1024:49152 + (ic + 1) * 1024],
                    in0=A[:, 49152 + ic * 1024:49152 + (ic + 1) * 1024],
                    scalar1=normw_c[:, ic % 2:ic % 2 + 1], scalar2=None,
                    op0=ALU.mult),
                   reads=("Wo", "normw"), writes=("Wo",))
            SC_(lambda e: e.activation(out=alog_c[0:8, 1:2],
                                       in_=alog_c[0:8, 0:1], func=AF.Exp),
                reads=("alog",), writes=("alogE",))
            V_(lambda e: e.tensor_scalar(out=alog_c[0:8, 1:2],
                                         in0=alog_c[0:8, 1:2], scalar1=-1.0,
                                         scalar2=None, op0=ALU.mult),
               reads=("alogE",), writes=("alogE",))
            V_(lambda e: e.memset(S132[:, 0:2048], 0.0),
               writes=tuple(f"S{h}" for h in range(8)))
            V_(lambda e: e.memset(B[:, 18432:20480], 0.0),
               writes=tuple(f"Sbf{h}" for h in range(8)))
            V_(lambda e: e.memset(convtail[:, 0:96], 0.0), writes=("ctail",))

            beta_hc = gch
            LNB = 0.0883883476483184395
            import math
            LN_DKS = math.log(LNB)

            if attn_stage < 0:
                return
            for b in range(NB):
                if attn_stage == 0.1 and b > 0:
                    break
                for cc in range(2):
                    i = 2 * b + cc
                    rt = rt0 if cc == 0 else rt1
                    rtk = f"rt{cc}"
                    ldma(rt[:], res_d[i * 128:(i + 1) * 128, :],
                         reads=(f"res{i}",), writes=(rtk,))
                    c, ck = layer_norm(rt, rtk, cc)
                    V_(lambda e: e.tensor_copy(out=xlnb[:], in_=xln[:]),
                       reads=("xln",), writes=("xlnb",))
                    for g0 in range(0, 8, 4):
                        ems = [MM(pb[2][:, j * 128:(j + 1) * 128],
                                  xlnb[:, (g0 + j) * 128:(g0 + j + 1) * 128],
                                  identb[:], True, True) for j in range(4)]
                        T_(ems, reads=("xlnb", "identb"), writes=("pb2",))
                        for j in range(4):
                            V_(lambda e, fc=g0 + j, j=j, cc=cc: e.tensor_copy(
                                out=xlnfm(fc)[:, cc * 128:(cc + 1) * 128],
                                in_=pb[2][:, j * 128:(j + 1) * 128]),
                               reads=("pb2",), writes=("xlnfm",))
                if attn_stage < 0.3:
                    continue
                ems = [MM(pb[3][0:8, 0:256],
                          wba_sb[:, fc * 16:fc * 16 + 8],
                          xlnfm(fc)[:], fc == 0, fc == 7) for fc in range(8)]
                ems += [MM(pb[3][0:8, 256:512],
                           wba_sb[:, fc * 16 + 8:fc * 16 + 16],
                           xlnfm(fc)[:], fc == 0, fc == 7) for fc in range(8)]
                T_(ems, reads=("xlnfm", "wba"), writes=("pb3",))
                SC_(lambda e: e.activation(out=zk[0:8, 768:1024],
                                           in_=pb[3][0:8, 0:256],
                                           func=AF.Exp, scale=-1.0),
                    reads=("pb3",), writes=("bsg", "zk_s1"))
                SC_(lambda e: e.activation(out=zk[0:8, 768:1024],
                                           in_=zk[0:8, 768:1024],
                                           func=AF.Ln, bias=1.0),
                    reads=("bsg",), writes=("bsg",))
                SC_(lambda e: e.activation(out=zk[0:8, 0:256],
                                           in_=zk[0:8, 768:1024],
                                           func=AF.Exp, scale=-1.0),
                    reads=("bsg",), writes=("beta8",))
                SC_(lambda e: e.activation(out=zk[0:8, 256:512],
                                           in_=pb[3][0:8, 256:512],
                                           func=AF.Exp,
                                           bias=alog_c[0:8, 2:3]),
                    reads=("pb3", "alog2"), writes=("g8e",))
                SC_(lambda e: e.activation(out=zk[0:8, 512:768],
                                           in_=zk[0:8, 256:512],
                                           func=AF.Ln, bias=1.0),
                    reads=("g8e",), writes=("g8l",))
                V_(lambda e: e.tensor_scalar(out=zk[0:8, 512:768],
                                             in0=zk[0:8, 512:768],
                                             scalar1=alog_c[0:8, 1:2],
                                             scalar2=None, op0=ALU.mult),
                   reads=("g8l", "alogE"), writes=("g8",))
                ldma(gch[:], bass.AP(zk, 512,
                                     [[1024, 8], [128, 2], [1, 128]]),
                     reads=("g8",), writes=("gch",))
                ldma(bech[:], bass.AP(zk, 0,
                                      [[1024, 8], [128, 2], [1, 128]]),
                     reads=("beta8",), writes=("bech",))
                V_(lambda e: e.tensor_tensor_scan(out=bch[:], data0=gch[:],
                                                  data1=gch[:], initial=0.0,
                                                  op0=ALU.add, op1=ALU.bypass),
                   reads=("gch",), writes=("bch",))
                V_(lambda e: e.tensor_scalar(out=ehat[:], in0=bch[:],
                                             scalar1=bch[:, 127:128],
                                             scalar2=None, op0=ALU.subtract),
                   reads=("bch",), writes=("ehatp",))
                SC_(lambda e: e.activation(out=ehat[:], in_=ehat[:],
                                           func=AF.Exp, scale=-1.0),
                    reads=("ehatp",), writes=("ehat",))
                for tsrc, tk, tdst, tdk in ((bch, "bch", bT, "bT"),
                                            (bech, "bech", betaT, "betaT"),
                                            (ehat, "ehat", ehatT2, "ehatT")):
                    T_([MM(pb[7][:, 0:16], tsrc[:], identf[0:16, 0:16],
                           True, True)],
                       reads=(tk, "identf"), writes=("pb7",))
                    V_(lambda e, d=tdst: e.tensor_copy(out=d[:],
                                                       in_=pb[7][:, 0:16]),
                       reads=("pb7",), writes=(tdk,))
                V_(lambda e: e.tensor_scalar(out=bnT[:], in0=bT[:],
                                             scalar1=-1.0, scalar2=None,
                                             op0=ALU.mult),
                   reads=("bT",), writes=("bnT",))
                SC_(lambda e: e.activation(out=gamT[:], in_=bT[:], func=AF.Exp),
                    reads=("bT",), writes=("gamT",))
                V_(lambda e: e.tensor_scalar(out=nbetaT[:], in0=betaT[:],
                                             scalar1=-1.0, scalar2=None,
                                             op0=ALU.mult),
                   reads=("betaT",), writes=("nbetaT",))
                V_(lambda e: e.tensor_mul(out=bgT[:], in0=betaT[:],
                                          in1=gamT[:]),
                   reads=("betaT", "gamT"), writes=("bgT",))
                ldma(sc_c[0:1, 16:32], bT[127:128, 0:16],
                     reads=("bT",), writes=("bCrow",))
                T_([MM(pb[7][:, 16:32], onesf[0:1, 0:128],
                       sc_c[0:1, 16:32], True, True)],
                   reads=("bCrow", "onesf"), writes=("pb7",))
                SC_(lambda e: e.activation(out=ebC[:], in_=pb[7][:, 16:32],
                                           func=AF.Exp),
                    reads=("pb7",), writes=("ebC",))

                if attn_stage < 0.5:
                    continue
                def proj_conv(kind, oc, wbase, cwbase):
                    bk = oc % 2
                    ems = [MM(pb[bk][:, 0:256],
                              A[:, wbase + fc * (1024 if kind != "v" else 2048)
                                + oc * 128:
                                wbase + fc * (1024 if kind != "v" else 2048)
                                + (oc + 1) * 128],
                              xlnfm(fc)[:], fc == 0, fc == 7)
                           for fc in range(8)]
                    wk_key = {"q": "Wq", "k": "Wk", "v": "Wv"}[kind]
                    T_(ems, reads=("xlnfm", wk_key), writes=(f"pb{bk}",))
                    ti = {"q": 0, "k": 1, "v": 2}[kind]
                    tail = convtail[:, (cwbase + oc) * 3:(cwbase + oc + 1) * 3]
                    SC_(lambda e: e.copy(out=xtr[:, 3:259],
                                         in_=pb[bk][:, 0:256]),
                        reads=(f"pb{bk}",), writes=("xtr",))
                    V_(lambda e, t=tail: e.tensor_copy(out=xtr[:, 0:3], in_=t),
                       reads=("ctail",), writes=("xtr3",))
                    V_(lambda e, t=tail: e.tensor_copy(out=t,
                                                       in_=xtr[:, 256:259]),
                       reads=("xtr",), writes=("ctail",))
                    cw = lambda tap: convw[:, (cwbase + oc) * 4 + tap:
                                           (cwbase + oc) * 4 + tap + 1]
                    eng = V_
                    eng(lambda e: e.tensor_scalar(
                        out=zk[:, 0:256], in0=xtr[:, 0:256], scalar1=cw(0),
                        scalar2=None, op0=ALU.mult),
                        reads=("xtr", "xtr3", "convw"),
                        writes=("cacc", "zk_s0", "beta8", "g8e"))
                    eng(lambda e: e.scalar_tensor_tensor(
                        out=zk[:, 0:256], in0=xtr[:, 1:257],
                        scalar=cw(1), in1=zk[:, 0:256],
                        op0=ALU.mult, op1=ALU.add),
                        reads=("xtr", "xtr3", "convw", "cacc"),
                        writes=("cacc",))
                    eng(lambda e: e.tensor_scalar(
                        out=zk[:, 256:512], in0=xtr[:, 2:258], scalar1=cw(2),
                        scalar2=None, op0=ALU.mult),
                        reads=("xtr", "xtr3", "convw"),
                        writes=("cacc2", "zk_s0", "g8e"))
                    eng(lambda e: e.scalar_tensor_tensor(
                        out=zk[:, 256:512], in0=xtr[:, 3:259],
                        scalar=cw(3), in1=zk[:, 256:512],
                        op0=ALU.mult, op1=ALU.add),
                        reads=("xtr", "xtr3", "convw", "cacc2"),
                        writes=("cacc2",))
                    eng(lambda e: e.tensor_tensor(
                        out=zk[:, 0:256], in0=zk[:, 0:256],
                        in1=zk[:, 256:512], op=ALU.add),
                        reads=("cacc", "cacc2"), writes=("cacc",))
                    zkb = zk[:].bitcast(BF16)
                    SC_(lambda e: e.activation(out=zk[:, 768:1024],
                                               in_=zk[:, 0:256],
                                               func=AF.Exp, scale=-1.0),
                        reads=("cacc",), writes=("bsg", "zk_s1"))
                    SC_(lambda e: e.activation(out=zk[:, 768:1024],
                                               in_=zk[:, 768:1024],
                                               func=AF.Ln, bias=1.0),
                        reads=("bsg",), writes=("bsg",))
                    SC_(lambda e: e.activation(out=zk[:, 768:1024],
                                               in_=zk[:, 768:1024],
                                               func=AF.Exp, scale=-1.0),
                        reads=("bsg",), writes=("bsg",))
                    V_(lambda e: e.tensor_tensor(out=zkb[:, 1024:1280],
                                                 in0=zk[:, 0:256],
                                                 in1=zk[:, 768:1024],
                                                 op=ALU.mult),
                       reads=("cacc", "bsg"),
                       writes=("sil", "zk_s1", "g8l", "g8"))
                    return zkb[:, 1024:1280]

                for h in range(H):
                    if attn_stage < 0.52 + 0.001 * h:
                        continue
                    sil = proj_conv("q", h, 0, 0)
                    if attn_stage < 0.54:
                        continue
                    for cc in range(2):
                        hc = h * 2 + cc
                        T_([MM(pb[2][:, 0:128], sil[:, cc * 128:(cc + 1) * 128],
                               identb[:], True, True)],
                           reads=("sil", "identb"), writes=("pb2",))
                        V_(lambda e: e.tensor_copy(out=ogt[:, 0:128],
                                                   in_=pb[2][:, 0:128]),
                           reads=("pb2",), writes=("P5",))
                        SC_(lambda e: e.activation(
                            out=sqd[:, 0:128], in_=ogt[:, 0:128],
                            func=AF.Square, accum_out=colB[:, 40:41]),
                            reads=("P5",), writes=("P1", "P1T", "ssq"))
                        SC_(lambda e: e.activation(out=colB[:, 41:42],
                                                   in_=colB[:, 40:41],
                                                   func=AF.Ln, bias=1e-6),
                            reads=("ssq",), writes=("ssl",))
                        SC_(lambda e: e.activation(out=colB[:, 42:43],
                                                   in_=colB[:, 41:42],
                                                   func=AF.Exp, scale=-0.5,
                                                   bias=LN_DKS),
                            reads=("ssl",), writes=("rstdq",))
                        V_(lambda e, hc=hc, h=h, cc=cc: e.tensor_scalar(
                            out=ogt[:, 0:128], in0=ogt[:, 0:128],
                            scalar1=colB[:, 42:43], scalar2=None,
                            op0=ALU.mult),
                           reads=("P5", "rstdq"), writes=("P5",))
                        T_([MM(pb[2][:, 128:256], ogt[:, 0:128], identb[:],
                               True, True)],
                           reads=("P5", "identb"), writes=("pb2",))
                        V_(lambda e, h=h, cc=cc: e.tensor_copy(
                            out=QTf(h, cc), in_=pb[2][:, 128:256]),
                           reads=("pb2",), writes=("QTf",))
                    if attn_stage < 0.56:
                        continue
                    sil = proj_conv("k", h, 8192, 8)
                    for cc in range(2):
                        hc = h * 2 + cc
                        T_([MM(pb[2][:, 0:128], sil[:, cc * 128:(cc + 1) * 128],
                               identb[:], True, True)],
                           reads=("sil", "identb"), writes=("pb2",))
                        V_(lambda e: e.tensor_copy(out=ogt[:, 0:128],
                                                   in_=pb[2][:, 0:128]),
                           reads=("pb2",), writes=("P5",))
                        SC_(lambda e: e.activation(
                            out=sqd[:, 0:128], in_=ogt[:, 0:128],
                            func=AF.Square, accum_out=colB[:, 43:44]),
                            reads=("P5",), writes=("P1", "P1T", "ssk"))
                        SC_(lambda e: e.activation(out=colB[:, 44:45],
                                                   in_=colB[:, 43:44],
                                                   func=AF.Ln, bias=1e-6),
                            reads=("ssk",), writes=("sskl",))
                        SC_(lambda e: e.activation(out=colB[:, 45:46],
                                                   in_=colB[:, 44:45],
                                                   func=AF.Exp, scale=-0.5),
                            reads=("sskl",), writes=("rstdk",))
                        V_(lambda e, hc=hc: e.tensor_tensor(
                            out=colB[:, 46:47], in0=colB[:, 45:46],
                            in1=bgT[:, hc:hc + 1], op=ALU.mult),
                           reads=("rstdk", "bgT"), writes=("cbk",))
                        V_(lambda e, hc=hc: e.tensor_tensor(
                            out=colB[:, 47:48], in0=colB[:, 45:46],
                            in1=ehatT2[:, hc:hc + 1], op=ALU.mult),
                           reads=("rstdk", "ehatT"), writes=("ckh",))
                        V_(lambda e, h=h, cc=cc: e.tensor_scalar(
                            out=BKt(h, cc), in0=ogt[:, 0:128],
                            scalar1=colB[:, 46:47], scalar2=None,
                            op0=ALU.mult),
                           reads=("P5", "cbk"), writes=("BKt",))
                        V_(lambda e, h=h, cc=cc: e.tensor_scalar(
                            out=Kh_(h, cc), in0=ogt[:, 0:128],
                            scalar1=colB[:, 47:48], scalar2=None,
                            op0=ALU.mult),
                           reads=("P5", "ckh"), writes=("Kh",))
                        V_(lambda e: e.tensor_scalar(
                            out=ogt[:, 0:128], in0=ogt[:, 0:128],
                            scalar1=colB[:, 45:46], scalar2=None,
                            op0=ALU.mult),
                           reads=("P5", "rstdk"), writes=("P5",))
                        T_([MM(pb[2][:, 128:256], ogt[:, 0:128], identb[:],
                               True, True)],
                           reads=("P5", "identb"), writes=("pb2",))
                        V_(lambda e, h=h, cc=cc: e.tensor_copy(
                            out=Kf_(h, cc), in_=pb[2][:, 128:256]),
                           reads=("pb2",), writes=("Kf",))
                    if attn_stage < 0.58:
                        continue
                    for dv in range(2):
                        sil = proj_conv("v", h * 2 + dv, 16384, 16)
                        for cc in range(2):
                            T_([MM(pb[2][:, 0:128],
                                   sil[:, cc * 128:(cc + 1) * 128],
                                   identb[:], True, True)],
                               reads=("sil", "identb"), writes=("pb2",))
                            V_(lambda e, h=h, cc=cc, dv=dv: e.tensor_copy(
                                out=Vb_(h, cc)[:, dv * 128:(dv + 1) * 128],
                                in_=pb[2][:, 0:128]),
                               reads=("pb2",), writes=("Vb",))
                    for cc in range(2):
                        hc = h * 2 + cc
                        V_(lambda e, h=h, cc=cc, hc=hc: e.tensor_scalar(
                            out=Vb_(h, cc), in0=Vb_(h, cc),
                            scalar1=betaT[:, hc:hc + 1], scalar2=None,
                            op0=ALU.mult),
                           reads=("Vb", "betaT"), writes=("Vb",))

                if attn_stage < 0.7:
                    continue
                for cc in range(2):
                    gidx = 2 * b + cc
                    rt = rt0 if cc == 0 else rt1
                    rtk = f"rt{cc}"
                    gvE = Ctx()
                    for qq in range(4):
                        bk = qq % 2
                        ems = [MM(pb[bk][:, 0:512],
                                  xlnfm(fc)[:, cc * 128:(cc + 1) * 128],
                                  A[:, 32768 + fc * 2048 + qq * 512:
                                    32768 + fc * 2048 + (qq + 1) * 512],
                                  fc == 0, fc == 7) for fc in range(8)]
                        gvE.T(ems, reads=("xlnfm", "Wgv"), writes=(f"pb{bk}",))
                        sgv = B32[:, 7680:8192]
                        gvE.SC(lambda e, bk=bk: e.activation(
                            out=sgv, in_=pb[bk][:, 0:512],
                            func=AF.Exp, scale=-1.0),
                            reads=(f"pb{bk}",), writes=("sgv",))
                        gvE.SC(lambda e: e.activation(out=sgv, in_=sgv,
                                                      func=AF.Ln, bias=1.0),
                                reads=("sgv",), writes=("sgv",))
                        gvE.SC(lambda e: e.activation(out=sgv, in_=sgv,
                                                      func=AF.Exp, scale=-1.0),
                                reads=("sgv",), writes=("sgv",))
                        gvE.V(lambda e, qq=qq, bk=bk: e.tensor_tensor(
                            out=B[:, 12288 + qq * 512:12288 + (qq + 1) * 512],
                            in0=pb[bk][:, 0:512], in1=sgv, op=ALU.mult),
                            reads=(f"pb{bk}", "sgv"), writes=("GV",))
                    qiB = qi[:].bitcast(BF16)
                    qi32 = qi[:].bitcast(F32)
                    yapI = lambda j: qiB[:, j * 128:(j + 1) * 128]
                    AqkI = lambda j: qiB[:, 256 + j * 128:256 + (j + 1) * 128]
                    VpS = qi32[:, 256:512]
                    qsb2 = S232[:, 1472:1728]
                    ogf2 = S2[:, 3584:3712]

                    def ph1_ops(h, cc=cc):
                        r = []
                        hc = h * 2 + cc
                        j = h % 2
                        def rT(ems, reads=(), writes=()):
                            r.append(("tensor", ems, reads, writes))
                        def rV(em, reads=(), writes=()):
                            r.append(("vector", [em], reads, writes))
                        def rS(em, reads=(), writes=()):
                            r.append(("scalar", [em], reads, writes))
                        cbc = S232[:, 704:832]
                        rV(lambda e, hc=hc: e.tensor_scalar(
                            out=cbc, in0=onesf128[:], scalar1=bnT[:, hc:hc + 1],
                            scalar2=None, op0=ALU.mult),
                           reads=("onesf128", "bnT"), writes=("P1", "P1T"))
                        rT([MM(pb[3][:, 0:128], cbc, identf[:], True, True)],
                           reads=("P1", "P1T", "identf"), writes=("pb3",))
                        rV(lambda e, hc=hc: e.scalar_tensor_tensor(
                            out=Dls[:], in0=pb[3][:, 0:128],
                            scalar=bT[:, hc:hc + 1], in1=mask_ksl[:],
                            op0=ALU.add, op1=ALU.add),
                           reads=("pb3", "bT", "mask_ksl"), writes=("DlsP",))
                        rS(lambda e: e.activation(out=Dls[:], in_=Dls[:],
                                                  func=AF.Exp),
                           reads=("DlsP",), writes=("Dls",))
                        rV(lambda e: e.scalar_tensor_tensor(
                            out=DAb[:], in0=Dls[:], scalar=1.0,
                            in1=identf[:], op0=ALU.mult, op1=ALU.add),
                           reads=("Dls", "identf"), writes=("P5T",))
                        rT([MM(pb[3][:, 128:256], DAb[:], identb[:],
                               True, True)],
                           reads=("P5T", "identb"), writes=("pb3",))
                        rS(lambda e: e.copy(out=DAt[:], in_=pb[3][:, 128:256]),
                           reads=("pb3",), writes=("DAt",))
                        rT([MM(pb[3][:, 256:384], Kf_(h, cc), Kf_(h, cc),
                               True, True)],
                           reads=("Kf",), writes=("pb3",))
                        rV(lambda e: e.tensor_copy(out=G32[:],
                                                   in_=pb[3][:, 256:384]),
                           reads=("pb3",), writes=("G32",))
                        rT([MM(pb[3][:, 384:512], Kf_(h, cc), QTf(h, cc),
                               True, True)],
                           reads=("Kf", "QTf"), writes=("pb3",))
                        rV(lambda e, j=j: e.scalar_tensor_tensor(
                            out=AqkI(j), in0=pb[3][:, 384:512], scalar=1.0,
                            in1=DAt[:], op0=ALU.mult, op1=ALU.mult),
                           reads=("pb3", "DAt"), writes=(f"AqkI{j}",))
                        rV(lambda e, hc=hc: e.scalar_tensor_tensor(
                            out=Nt_[:], in0=G32[:],
                            scalar=nbetaT[:, hc:hc + 1], in1=Dls[:],
                            op0=ALU.mult, op1=ALU.mult),
                           reads=("G32", "nbetaT", "Dls"), writes=("N",))
                        rT([MM(pb[4][:, 0:128], Nt_[:], identb[:],
                               True, True)],
                           reads=("N", "identb"), writes=("pb4",))
                        rS(lambda e: e.copy(out=NTt[:], in_=pb[4][:, 0:128]),
                           reads=("pb4",), writes=("NT",))
                        def SQ2(lhs1, lk1, rhs1, rk1, lhs2, lk2, rhs2, rk2,
                                slot, k1, k2):
                            rT([MM(pb[4][:, 128:256], lhs1, rhs1, True, True),
                                MM(pb[4][:, 256:384], lhs2, rhs2, True, True)],
                               reads=(lk1, rk1, lk2, rk2), writes=("pb4",))
                            rS(lambda e, s=slot: e.copy(
                                out=S2[:, 1408 + s * 128:1408 + (s + 2) * 128],
                                in_=pb[4][:, 128:384]),
                               reads=("pb4",), writes=(k1, k2))
                        rV(lambda e: e.tensor_tensor(
                            out=Yc(0), in0=identb[:], in1=NTt[:], op=ALU.add),
                           reads=("NT", "identb"), writes=("Y0",))
                        sq_args = [
                            (NTt[:], "NT", Nt_[:], "N",
                             Nt_[:], "N", NTt[:], "NT", 0, "P1", "P1T"),
                            (Pw(1), "P1T", Pw(0), "P1",
                             Pw(0), "P1", Pw(1), "P1T", 2, "P2", "P2T"),
                            (Pw(3), "P2T", Pw(2), "P2",
                             Pw(2), "P2", Pw(3), "P2T", 4, "P3", "P3T"),
                            (Pw(5), "P3T", Pw(4), "P3",
                             Pw(4), "P3", Pw(5), "P3T", 6, "P4", "P4T"),
                            (Pw(7), "P4T", Pw(6), "P4",
                             Pw(6), "P4", Pw(7), "P4T", 8, "P5", "P5T"),
                        ]
                        ysrc = [(Pw(0), "P1"), (Pw(2), "P2"), (Pw(4), "P3"),
                                (Pw(6), "P4"), (Pw(8), "P5")]
                        yi = 0
                        for lev in range(5):
                            SQ2(*sq_args[lev])
                            lp, lk = ysrc[lev]
                            rT([MM(pb[3][:, 384:512], lp, Yc(yi), True, True)],
                               reads=(lk, f"Y{yi}"), writes=("pb3",))
                            rV(lambda e, yi=yi: e.tensor_tensor(
                                out=Yc(1 - yi), in0=pb[3][:, 384:512],
                                in1=Yc(yi), op=ALU.add),
                               reads=("pb3", f"Y{yi}"), writes=(f"Y{1 - yi}",))
                            yi = 1 - yi
                        rT([MM(pb[4][:, 384:512], Pw(9), Pw(8), True, True)],
                           reads=("P5", "P5T"), writes=("pb4",))
                        rS(lambda e: e.copy(out=Pw(0), in_=pb[4][:, 384:512]),
                           reads=("pb4",), writes=("P6", "P1"))
                        rT([MM(pb[3][:, 384:512], Pw(0), Yc(yi), True, True)],
                           reads=("P6", "P1", f"Y{yi}"), writes=("pb3",))
                        rV(lambda e, yi=yi: e.tensor_tensor(
                            out=Yc(1 - yi), in0=pb[3][:, 384:512],
                            in1=Yc(yi), op=ALU.add),
                           reads=("pb3", f"Y{yi}"), writes=(f"Y{1 - yi}",))
                        rV(lambda e, yi=yi, j=j: e.tensor_copy(
                            out=yapI(j), in_=Yc(1 - yi)),
                           reads=(f"Y{1 - yi}",), writes=(f"YI{j}",))
                        return r

                    def ph2_ops(h, cc=cc):
                        r = []
                        hc = h * 2 + cc
                        j = h % 2
                        def rT(ems, reads=(), writes=()):
                            r.append(("tensor", ems, reads, writes))
                        def rV(em, reads=(), writes=()):
                            r.append(("vector", [em], reads, writes))
                        def rS(em, reads=(), writes=()):
                            r.append(("scalar", [em], reads, writes))
                        rT([MM(pb[5][:, 0:256], yapI(j), Vb_(h, cc),
                               True, True),
                            MM(pb[5][:, 256:384], BKt(h, cc), yapI(j),
                               True, True)],
                           reads=(f"YI{j}", "Vb", "BKt"), writes=("pb5",))
                        rS(lambda e: e.copy(out=WT_[:], in_=pb[5][:, 256:384]),
                           reads=("pb5",), writes=("WTk",))
                        rS(lambda e: e.copy(out=VpS, in_=pb[5][:, 0:256]),
                           reads=("pb5",), writes=("Vp",))
                        rT([MM(pb[6][:, 0:256], WT_[:], Sbf(h), True, True)],
                           reads=("WTk", f"Sbf{h}"), writes=("pb6",))
                        rV(lambda e: e.scalar_tensor_tensor(
                            out=U_[:], in0=pb[6][:, 0:256], scalar=-1.0,
                            in1=VpS, op0=ALU.mult, op1=ALU.add),
                           reads=("Vp", "pb6"), writes=("U",))
                        rT([MM(pb[6][:, 256:512], QTf(h, cc), Sbf(h),
                               True, True)],
                           reads=("QTf", f"Sbf{h}"), writes=("pb6",))
                        rV(lambda e, hc=hc: e.tensor_scalar(
                            out=qsb2, in0=pb[6][:, 256:512],
                            scalar1=gamT[:, hc:hc + 1], scalar2=None,
                            op0=ALU.mult),
                           reads=("pb6", "gamT"), writes=("qsb",))
                        rT([MM(pb[6][:, 256:512], AqkI(j), U_[:], True, True)],
                           reads=(f"AqkI{j}", "U"), writes=("pb6",))
                        rV(lambda e: e.tensor_tensor(
                            out=qsb2, in0=pb[6][:, 256:512], in1=qsb2,
                            op=ALU.add),
                           reads=("qsb", "pb6"), writes=("qsb",))
                        rT([MM(pb[7][:, 256:512], Kh_(h, cc), U_[:],
                               True, True)],
                           reads=("Kh", "U"), writes=("pb7",))
                        rV(lambda e, h=h, hc=hc: e.tensor_scalar(
                            out=Sst(h), in0=Sst(h),
                            scalar1=ebC[:, hc:hc + 1], scalar2=None,
                            op0=ALU.mult),
                           reads=(f"S{h}", "ebC"), writes=(f"S{h}",))
                        rV(lambda e, h=h: e.tensor_tensor(
                            out=Sst(h), in0=pb[7][:, 256:512], in1=Sst(h),
                            op=ALU.add),
                           reads=(f"S{h}", "pb7"), writes=(f"S{h}",))
                        rS(lambda e, h=h: e.copy(out=Sbf(h), in_=Sst(h)),
                           reads=(f"S{h}",), writes=(f"Sbf{h}",))
                        rS(lambda e: e.activation(
                            out=pb[7][:, 0:256], in_=qsb2, func=AF.Square,
                            accum_out=colB[:, 48:49]),
                           reads=("qsb",), writes=("pb7", "sso"))
                        rV(lambda e: e.tensor_scalar(
                            out=colB[:, 49:50], in0=colB[:, 48:49],
                            scalar1=float(1.0 / DV), scalar2=1e-5,
                            op0=ALU.mult, op1=ALU.add),
                           reads=("sso",), writes=("ssoc",))
                        rS(lambda e: e.activation(out=colB[:, 50:51],
                                                  in_=colB[:, 49:50],
                                                  func=AF.Ln),
                           reads=("ssoc",), writes=("ssol",))
                        rS(lambda e: e.activation(out=colB[:, 51:52],
                                                  in_=colB[:, 50:51],
                                                  func=AF.Exp, scale=-0.5),
                           reads=("ssol",), writes=("rstdo",))
                        rV(lambda e: e.tensor_scalar(
                            out=qsb2, in0=qsb2, scalar1=colB[:, 51:52],
                            scalar2=None, op0=ALU.mult),
                           reads=("qsb", "rstdo"), writes=("qsb",))
                        rV(lambda e, cc=cc, h=h: e.tensor_tensor(
                            out=og2[:], in0=qsb2, in1=GVs(cc, h),
                            op=ALU.mult),
                           reads=("qsb", "GV"), writes=("U",))
                        for dv in range(2):
                            rT([MM(pb[2][:, 0:128],
                                   og2[:, dv * 128:(dv + 1) * 128],
                                   identb[:], True, True)],
                               reads=("U", "identb"), writes=("pb2",))
                            rV(lambda e: e.tensor_copy(
                                out=ogf2, in_=pb[2][:, 0:128]),
                               reads=("pb2",), writes=("ogf",))
                            ic = h * 2 + dv
                            ems = [MM(pb[hh][:, 0:512], ogf2,
                                      A[:, 49152 + ic * 1024 + hh * 512:
                                        49152 + ic * 1024 + hh * 512 + 512],
                                      ic == 0, ic == 15) for hh in range(2)]
                            rT(ems, reads=("ogf", "Wo"),
                               writes=("pb0", "pb1"))
                        return r

                    if attn_stage >= 1:
                        emit_ctx(merge2(gvE.l, ph1_ops(0)))
                        for h in range(1, H):
                            emit_ctx(merge2(ph2_ops(h - 1), ph1_ops(h)))
                        emit_ctx(ph2_ops(H - 1))
                    else:
                        emit_ctx(gvE.l)
                    for hh in range(2):
                        if attn_stage < 1:
                            break
                        V_(lambda e, hh=hh, r=rt: e.tensor_tensor(
                            out=r[:, hh * 512:(hh + 1) * 512],
                            in0=pb[hh][:, 0:512],
                            in1=r[:, hh * 512:(hh + 1) * 512],
                            op=ALU.add),
                           reads=(f"pb{hh}", rtk), writes=(rtk,))
                    ldma(res_d[gidx * 128:(gidx + 1) * 128, :], rt[:],
                         reads=(rtk,), writes=(f"res{gidx}",))

        if do_ffn1:
            ffn_sweep1(x_d, 0)
            tp.barrier()
            ffn_sweep2(x_d, res_d)
            tp.barrier()
        else:
            ldma(res_d, x_d,
                 writes=tuple(f"res{i}" for i in range(NT)))
            tp.barrier()

        if do_attn:
            build_attn()
            tp.barrier()

        if do_ffn2:
            ldma(A_gate, tern_g_d.rearrange("(a p) c -> p a c", a=8, p=128),
                 reads=("terng_dram",), writes=("terng_t",))
            ldma(A_up, tern_u_d.rearrange("(a p) c -> p a c", a=8, p=128),
                 reads=("ternu_dram",), writes=("ternu_t",))
            tp.barrier()
            ffn_sweep1(res_d, 1)
            tp.barrier()
            ffn_sweep2(res_d, out_d)
        else:
            tp.barrier()
            for i in range(NT):
                par = i % 2
                rt = rt0 if par == 0 else rt1
                ldma(rt[:], res_d[i * 128:(i + 1) * 128, :],
                     reads=(f"res{i}",), writes=(f"cp{par}",))
                ldma(out_d[i * 128:(i + 1) * 128, :], rt[:],
                     reads=(f"cp{par}",), writes=(f"cpo{par}",))

        if dbg:
            dbgA = dt("dbgA", [128, 65536], BF16, kind="ExternalOutput").ap()
            ldma(dbgA, A[:], writes=("dbgA",))
            dbgB = dt("dbgB", [128, 20480], BF16, kind="ExternalOutput").ap()
            dbgS1 = dt("dbgS1", [128, 4096], BF16, kind="ExternalOutput").ap()
            dbgS2 = dt("dbgS2", [128, 4096], BF16, kind="ExternalOutput").ap()
            tp.barrier()
            ldma(dbgB, B[:], writes=("dbgB",))
            ldma(dbgS1, S1[:], writes=("dbgS1",))
            ldma(dbgS2, S2[:], writes=("dbgS2",))

        tp.barrier()
        for li in range(NDMALANES):
            tp.wait("sync", f"s_d{li}", tp.val(f"s_d{li}"))

        block = es.enter_context(nc.Block())

        @block.sync
        def _(e):
            for f in tp.ops["sync"]:
                f(e)

        @block.tensor
        def _(e):
            for f in tp.ops["tensor"]:
                f(e)

        @block.vector
        def _(e):
            for f in tp.ops["vector"]:
                f(e)

        @block.scalar
        def _(e):
            for f in tp.ops["scalar"]:
                f(e)

        @block.gpsimd
        def _(e):
            for f in tp.ops["gpsimd"]:
                f(e)

    nc.finalize()
    return nc



def _consts():
    i = np.arange(128)
    mk = lambda keep: np.where(keep, np.float32(0.0),
                               np.float32(NEG)).astype(np.float32)
    return {
        "identb": np.eye(128, dtype=BFNP),
        "identf": np.eye(128, dtype=np.float32),
        "onesf": np.ones((1, 128), np.float32),
        "monesf": -np.ones((1, 128), np.float32),
        "onesf128": np.ones((128, 128), np.float32),
        "cvals": np.tile(np.asarray(CONSTVALS, np.float32)[None, :],
                         (128, 1)),
        "mask_ksl": mk(i[None, :] < i[:, None]),
    }


def _prep_shared(p):
    c = _consts()
    nw = np.asarray(p["L0_attn_norm_w"], np.float32).reshape(2, 128).T
    shared = {
        "wg_n": np.ascontiguousarray(p["L0_ffn_w_gate"], dtype=np.float32),
        "wu_n": np.ascontiguousarray(p["L0_ffn_w_up"], dtype=np.float32),
        "wd_n": np.ascontiguousarray(p["L0_ffn_w_down"], dtype=np.float32),
        "wg_t": np.ascontiguousarray(np.asarray(p["L0_ffn_w_gate"], np.float32).T),
        "wu_t": np.ascontiguousarray(np.asarray(p["L0_ffn_w_up"], np.float32).T),
        "wd_t": np.ascontiguousarray(np.asarray(p["L0_ffn_w_down"], np.float32).T),
        "wq": np.ascontiguousarray(p["L0_attn_Wq"], dtype=np.float32),
        "wk": np.ascontiguousarray(p["L0_attn_Wk"], dtype=np.float32),
        "wv": np.ascontiguousarray(p["L0_attn_Wv"], dtype=np.float32),
        "wgv": np.ascontiguousarray(p["L0_attn_Wg"], dtype=np.float32),
        "wo": np.ascontiguousarray(p["L0_attn_Wo"], dtype=np.float32),
        "wba": np.ascontiguousarray(np.concatenate(
            [np.asarray(p["L0_attn_Wb"], np.float32),
             np.asarray(p["L0_attn_Wa"], np.float32)], axis=1)),
        "convq": np.ascontiguousarray(p["L0_attn_conv_q"], dtype=np.float32),
        "convk": np.ascontiguousarray(p["L0_attn_conv_k"], dtype=np.float32),
        "convv": np.ascontiguousarray(p["L0_attn_conv_v"], dtype=np.float32),
        "alog": np.ascontiguousarray(
            np.asarray(p["L0_attn_A_log"], np.float32).reshape(8, 1)),
        "dtb": np.ascontiguousarray(
            np.asarray(p["L0_attn_dt_bias"], np.float32).reshape(8, 1)),
        "normw": np.ascontiguousarray(nw),
        **c,
    }
    return shared


def kernel(**inputs):
    x = np.ascontiguousarray(np.asarray(inputs["L0_attn_x"], np.float32))
    shared = _prep_shared(inputs)
    nc = build_core(T=4096)
    in_maps = [dict(shared, x=np.ascontiguousarray(x[b])) for b in range(8)]
    res = run_bass_kernel_spmd(nc, in_maps, core_ids=list(range(8)))
    return np.stack([res.results[b]["out"] for b in range(8)], axis=0)


if __name__ == "__main__":
    nc = build_core(T=4096)
    print("instructions:", sum(1 for _ in nc.all_instructions()))


# version 10
# speedup vs baseline: 1.0102x; 1.0102x over previous
import numpy as np
import ml_dtypes
from contextlib import ExitStack

import math
import concourse.bass as bass
import concourse.mybir as mybir
from concourse.bass_utils import run_bass_kernel_spmd

F32 = mybir.dt.float32
BF16 = mybir.dt.bfloat16
I32 = mybir.dt.int32
AF = mybir.ActivationFunctionType
ALU = mybir.AluOpType
AX = mybir.AxisListType
BFNP = ml_dtypes.bfloat16

D = 1024
INNER = 4096
LNB_DKS = math.log(1.0 / math.sqrt(128.0))
CONSTVALS = [1e-6, 1.0, -0.5, -1.0, LNB_DKS, 1e-5, 0.5, 0.0]
H, DK, DV = 8, 128, 256
NEG = -1.0e30

ENGINES = ("sync", "tensor", "vector", "scalar", "gpsimd")
ESEM = {"sync": "s_sy", "tensor": "s_t", "vector": "s_v", "scalar": "s_sc",
        "gpsimd": "s_g"}
NDMALANES = 12


class Tape:

    def __init__(self, nc):
        self.nc = nc
        self.ops = {e: [] for e in ENGINES}
        self.sems = {}
        self.cnt = {}
        self.writer = {}
        self.readers = {}
        self.waited = {e: {} for e in ENGINES}
        self.lane_next = 0
        self.lane_cnt = [0] * NDMALANES

    def bump(self, sem, n=1):
        self.cnt[sem] = self.cnt.get(sem, 0) + n
        return self.cnt[sem]

    def val(self, sem):
        return self.cnt.get(sem, 0)

    def wait(self, eng, sem, val):
        if val is None or val <= 0:
            return
        if self.waited[eng].get(sem, 0) >= val:
            return
        self.waited[eng][sem] = val
        self.ops[eng].append(lambda e, s=sem, v=val: e.wait_ge(self.sems[s], v))

    def _edges(self, eng, reads, writes):
        own = ESEM[eng]
        for k in reads:
            w = self.writer.get(k)
            if w is not None:
                self.wait(eng, w[0], w[1])
        for k in writes:
            for s, v in self.readers.get(k, {}).items():
                if s == own:
                    continue
                self.wait(eng, s, v)
            w = self.writer.get(k)
            if w is not None and w[0] != own:
                self.wait(eng, w[0], w[1])

    def _record(self, point, reads, writes):
        for k in writes:
            self.writer[k] = point
            self.readers[k] = {}
        for k in reads:
            self.readers.setdefault(k, {})
            self.readers[k][point[0]] = max(self.readers[k].get(point[0], 0),
                                            point[1])

    def group(self, eng, emits, reads=(), writes=(), dma=False, big=True):
        if not isinstance(emits, (list, tuple)):
            emits = [emits]
        pbs = tuple(k for k in reads if k.startswith("pb"))
        if pbs:
            writes = tuple(writes) + pbs
        self._edges(eng, reads, writes)
        if dma and eng != "gpsimd":
            assert len(emits) == 1
            lane = self.lane_next
            self.lane_next = (lane + 1) % NDMALANES
            sem = f"s_d{lane}"
            self.wait(eng, sem, 16 * self.lane_cnt[lane])
            self.lane_cnt[lane] += 1
            val = 16 * self.lane_cnt[lane]
            self.cnt[sem] = val
            inc = 16
        else:
            sem = "s_gd" if dma else ESEM[eng]
            inc = 16 if dma else 1
            val = self.bump(sem, inc)
        for f in emits[:-1]:
            self.ops[eng].append(f)
        self.ops[eng].append(
            lambda e, f=emits[-1], s=sem, i=inc: f(e).then_inc(self.sems[s], i))
        self._record((sem, val), reads, writes)
        return (sem, val)

    def op(self, eng, emit, reads=(), writes=(), dma=False, big=True):
        return self.group(eng, [emit], reads, writes, dma=dma, big=big)

    def barrier(self):
        snap = dict(self.cnt)
        for eng in ENGINES:
            for s, v in snap.items():
                if s == ESEM[eng]:
                    continue
                self.wait(eng, s, v)


def _need(d, k):
    v = d.get(k)
    assert v is not None, k
    return v


def build_core(T=4096, do_ffn1=True, do_attn=True, do_ffn2=True,
               attn_nblocks=None, attn_stage=2, dbg=False):
    NT = T // 128
    nc = bass.Bass("TRN2", target_bir_lowering=False,
                   detect_race_conditions=False,
                   dynamic_dma_scratch_size=8192)
    tp = Tape(nc)
    dt = nc.dram_tensor

    x_d = dt("x", [T, D], F32, kind="ExternalInput").ap()
    out_d = dt("out", [T, D], F32, kind="ExternalOutput").ap()
    wg_n = dt("wg_n", [INNER, D], F32, kind="ExternalInput").ap()
    wu_n = dt("wu_n", [INNER, D], F32, kind="ExternalInput").ap()
    wd_n = dt("wd_n", [D, INNER], F32, kind="ExternalInput").ap()
    wg_t = dt("wg_t", [D, INNER], F32, kind="ExternalInput").ap()
    wu_t = dt("wu_t", [D, INNER], F32, kind="ExternalInput").ap()
    wd_t = dt("wd_t", [INNER, D], F32, kind="ExternalInput").ap()
    wq_d = dt("wq", [D, D], F32, kind="ExternalInput").ap()
    wk_d = dt("wk", [D, D], F32, kind="ExternalInput").ap()
    wv_d = dt("wv", [D, 2 * D], F32, kind="ExternalInput").ap()
    wgv_d = dt("wgv", [D, 2 * D], F32, kind="ExternalInput").ap()
    wo_d = dt("wo", [2 * D, D], F32, kind="ExternalInput").ap()
    wba_d = dt("wba", [D, 16], F32, kind="ExternalInput").ap()
    convq_d = dt("convq", [D, 4], F32, kind="ExternalInput").ap()
    convk_d = dt("convk", [D, 4], F32, kind="ExternalInput").ap()
    convv_d = dt("convv", [2 * D, 4], F32, kind="ExternalInput").ap()
    alog_d = dt("alog", [8, 1], F32, kind="ExternalInput").ap()
    dtb_d = dt("dtb", [8, 1], F32, kind="ExternalInput").ap()
    normw_d = dt("normw", [128, 2], F32, kind="ExternalInput").ap()
    cvals_d = dt("cvals", [128, 8], F32, kind="ExternalInput").ap()
    identb_d = dt("identb", [128, 128], BF16, kind="ExternalInput").ap()
    onesf_d = dt("onesf", [1, 128], F32, kind="ExternalInput").ap()
    monesf_d = dt("monesf", [1, 128], F32, kind="ExternalInput").ap()
    mask_ksl_d = dt("mask_ksl", [128, 128], F32, kind="ExternalInput").ap()
    onesf128_d = dt("onesf128", [128, 128], F32, kind="ExternalInput").ap()
    identf_d = dt("identf", [128, 128], F32, kind="ExternalInput").ap()

    tern_g_d = dt("tern_g", [D, INNER], BF16, kind="Internal").ap()
    tern_u_d = dt("tern_u", [D, INNER], BF16, kind="Internal").ap()
    tern_d_d = dt("tern_d", [INNER, D], BF16, kind="Internal").ap()
    res_d = dt("res_dram", [T, D], F32, kind="Internal").ap()
    hq_d = dt("hq_dram", [T, INNER], BF16, kind="Internal").ap()

    es = ExitStack()
    SB = lambda n, s, d: es.enter_context(nc.sbuf_tensor(n, s, d))
    PS = lambda n, s, d: es.enter_context(nc.psum_tensor(n, s, d))

    with es:
        A = SB("arenaA", [128, 65536], BF16)
        B = SB("arenaB", [128, 20480], BF16)
        B32 = B[:].bitcast(F32)

        rt0 = SB("rt0", [128, 1024], F32)
        rt1 = SB("rt1", [128, 1024], F32)
        xln = SB("xln", [128, 1024], F32)
        zk = SB("zk", [128, 1024], F32)
        qi = SB("qi", [128, 512], I32)
        S1 = SB("S1", [128, 4096], BF16)
        S132 = S1[:].bitcast(F32)
        xqb0 = S1[:, 0:1024]
        xqb1 = S1[:, 1024:2048]
        xqf0 = S1[:, 2048:3072]
        xqf1 = S1[:, 3072:4096]
        S2 = SB("S2", [128, 4096], BF16)
        S232 = S2[:].bitcast(F32)
        colA = SB("colA", [128, 16], F32)
        colB = SB("colB", [128, 64], F32)
        hsc = SB("hsc", [128, 32], F32)
        wst = SB("wst", [128, 32], F32)
        stt_ = SB("stt_", [128, 8], F32)
        sc_c = SB("sc_c", [1, 32], F32)
        identb = SB("identb_s", [128, 128], BF16)
        onesf = SB("onesf_s", [1, 128], F32)
        monesf = SB("monesf_s", [1, 128], F32)
        mask_ksl = SB("mask_ksl_s", [128, 128], F32)
        onesf128 = SB("onesf128_s", [128, 128], F32)
        identf = SB("identf_s", [128, 128], F32)
        normw_c = SB("normw_c", [128, 2], F32)
        cvals = SB("cvals_s", [128, 8], F32)
        for _j, _v in enumerate(CONSTVALS):
            nc.const_aps.register(F32, float(_v), cvals[:, _j:_j + 1])
        convw = SB("convw", [128, 128], F32)
        wba_sb = SB("wba_sb", [128, 128], BF16)
        wba_f = SB("wba_f", [128, 128], F32)
        xtr = SB("xtr", [128, 264], F32)
        alog_c = SB("alog_c", [8, 4], F32)
        gch = SB("gch", [16, 128], F32)
        bch = SB("bch", [16, 128], F32)
        bech = SB("bech", [16, 128], F32)
        ehat = SB("ehat", [16, 128], F32)
        bT = SB("bT", [128, 16], F32)
        bnT = SB("bnT", [128, 16], F32)
        betaT = SB("betaT", [128, 16], F32)
        nbetaT = SB("nbetaT", [128, 16], F32)
        gamT = SB("gamT", [128, 16], F32)
        bgT = SB("bgT", [128, 16], F32)
        ehatT2 = SB("ehatT2", [128, 16], F32)
        ebC = SB("ebC", [128, 16], F32)
        convtail = SB("convtail", [128, 96], F32)

        pb = [PS(f"pb{i}", [128, 512], F32) for i in range(8)]

        for s in sorted(set(ESEM.values()) | {"s_gd"}
                        | {f"s_d{i}" for i in range(NDMALANES)}):
            tp.sems[s] = es.enter_context(nc.semaphore(s))

        def _sbparts(ap):
            try:
                if type(ap.tensor).__name__ == "SBTensorHandle":
                    return int(ap.shape[0])
            except Exception:
                pass
            return None

        def ldma(dst, src, reads=(), writes=(), eng="sync", drain=False):
            return tp.op(eng, lambda e, d=dst, s=src: e.dma_start(out=d, in_=s),
                         reads=reads, writes=writes, dma=True)

        def T_(emits, reads=(), writes=()):
            return tp.group("tensor", emits, reads, writes)

        def V_(emits, reads=(), writes=()):
            return tp.group("vector", emits, reads, writes)

        def SC_(emits, reads=(), writes=()):
            return tp.group("scalar", emits, reads, writes)

        def G_(emits, reads=(), writes=()):
            return tp.group("gpsimd", emits, reads, writes)

        def MM(out_ap, lhsT_ap, rhs_ap, start, stop):
            return (lambda e, o=out_ap, l=lhsT_ap, r=rhs_ap, a=start, b=stop:
                    e.matmul(out=o, lhsT=l, rhs=r, start=a, stop=b))

        ldma(identb[:], identb_d, writes=("identb",))
        ldma(onesf[:], onesf_d, writes=("onesf",))
        ldma(monesf[:], monesf_d, writes=("monesf",))
        ldma(mask_ksl[:], mask_ksl_d, writes=("mask_ksl",))
        ldma(onesf128[:], onesf128_d, writes=("onesf128",))
        ldma(identf[:], identf_d, writes=("identf",))
        ldma(normw_c[:], normw_d, writes=("normw",))
        ldma(cvals[:], cvals_d, writes=("cvals",))
        ldma(alog_c[0:8, 0:1], alog_d, writes=("alog",))
        ldma(alog_c[0:8, 2:3], dtb_d, writes=("alog2",))

        def weight_stats(w_nat, big_rows, tag, c_slot):
            w1_dram = dt(f"w1_{tag}", [128, 32] if big_rows else [32, 128],
                         F32, kind="Internal").ap()
            w2_dram = dt(f"w2_{tag}", [4, 1], F32, kind="Internal").ap()
            if big_rows:
                src = w_nat.rearrange("(R r) (G i c) -> R G i r c",
                                      R=128, r=32, G=8, i=4, c=32)
            else:
                src = None
            for ps in range(8):
                bufk = "Ba" if ps % 2 == 0 else "Bb"
                buf = B32[:, (ps % 2) * 4096:(ps % 2) * 4096 + 4096]
                for iw in range(4):
                    if big_rows:
                        s_ap = bass.AP(w_nat.tensor,
                                       (ps * 4 + iw) * 32,
                                       [[32768, 128], [1024, 32], [1, 32]])
                        ldma(buf[:, iw * 1024:(iw + 1) * 1024]
                             .rearrange("p (r c) -> p r c", r=32, c=32),
                             s_ap, writes=(bufk,), drain=True)
                    else:
                        for gg in range(4):
                            s_ap = bass.AP(
                                w_nat.tensor,
                                gg * 1024 + (ps * 4 + iw) * 32,
                                [[131072, 32], [4096, 32], [1, 32]])
                            ldma(buf[gg * 32:(gg + 1) * 32,
                                     iw * 1024:(iw + 1) * 1024]
                                 .rearrange("p (r c) -> p r c", r=32, c=32),
                                 s_ap, writes=(bufk,))
                V_(lambda e, b=buf, ps=ps: e.tensor_reduce(
                    out=wst[:, ps * 4:(ps + 1) * 4]
                        .rearrange("p (a u) -> p a u", a=4, u=1),
                    in_=b.rearrange("p (a b c) -> p a b c", a=4, b=32, c=32),
                    axis=AX.XY, op=ALU.add, apply_absolute_value=True),
                   reads=(bufk,), writes=(f"wst{ps}",))
            if big_rows:
                ldma(w1_dram, wst[:, 0:32],
                     reads=tuple(f"wst{j}" for j in range(8)), writes=("w1",))
                ldma(B32[0:4, 0:1024],
                     w1_dram.rearrange("(a r) c -> a (r c)", a=4, r=32),
                     reads=("w1",), writes=("stg2", "Ba"))
            else:
                ldma(bass.AP(w1_dram.tensor, 0,
                             [[32, 4], [128, 32], [1, 32]]),
                     wst[:, 0:32],
                     reads=tuple(f"wst{j}" for j in range(8)), writes=("w1",))
                ldma(B32[0:4, 0:1024],
                     bass.AP(w1_dram.tensor, 0,
                             [[32, 4], [128, 32], [1, 32]]),
                     reads=("w1",), writes=("stg2", "Ba"))
            V_(lambda e: e.tensor_reduce(out=stt_[0:4, 0:1],
                                         in_=B32[0:4, 0:1024],
                                         axis=AX.X, op=ALU.add),
               reads=("stg2",), writes=("stg3",))
            ldma(w2_dram, stt_[0:4, 0:1], reads=("stg3",), writes=("w2",))
            ldma(sc_c[0:1, 8:12],
                 w2_dram.rearrange("(u a) c -> u (a c)", u=1, a=4),
                 reads=("w2",), writes=("stg4",))
            V_(lambda e: e.tensor_reduce(out=sc_c[0:1, 12:13],
                                         in_=sc_c[0:1, 8:12],
                                         axis=AX.X, op=ALU.add),
               reads=("stg4",), writes=("stg5",))
            V_(lambda e, cs=c_slot: e.tensor_scalar(
                out=sc_c[0:1, cs:cs + 1], in0=sc_c[0:1, 12:13],
                scalar1=float(1.0 / 4194304.0), scalar2=1e-5,
                op0=ALU.mult, op1=ALU.max),
               reads=("stg5",), writes=(f"c_{tag}",))
            V_(lambda e, cs=c_slot: e.reciprocal(
                out=sc_c[0:1, cs + 1:cs + 2], in_=sc_c[0:1, cs:cs + 1]),
               reads=(f"c_{tag}",), writes=(f"s_{tag}",))

        weight_stats(wg_n, True, "g", 0)
        weight_stats(wu_n, True, "u", 2)
        weight_stats(wd_n, False, "d", 4)

        def bcast_col(src_ap, dst_col, rkey, wkey):
            T_([MM(pb[7][:, 0:1], onesf[0:1, :], src_ap, True, True)],
               reads=("onesf", rkey), writes=("pb7",))
            V_(lambda e, d=dst_col: e.tensor_copy(out=d, in_=pb[7][:, 0:1]),
               reads=("pb7",), writes=(wkey,))

        bcast_col(sc_c[0:1, 1:2], colA[:, 0:1], "s_g", "colA_sg")
        bcast_col(sc_c[0:1, 3:4], colA[:, 1:2], "s_u", "colA_su")
        bcast_col(sc_c[0:1, 5:6], colA[:, 2:3], "s_d", "colA_sd")
        bcast_col(sc_c[0:1, 0:1], colA[:, 3:4], "c_g", "colA_cg")
        bcast_col(sc_c[0:1, 2:3], colA[:, 4:5], "c_u", "colA_cu")
        bcast_col(sc_c[0:1, 4:5], colA[:, 5:6], "c_d", "colA_cd")

        def ternarize(w_t_fp32, scale_col, scol_key, dst_sb, dst_key,
                      dst_dram, nrt, ncol):
            for rt in range(nrt):
                bufk = "Ba" if rt % 2 == 0 else "Bb"
                buf = B32[:, (rt % 2) * 4096:(rt % 2) * 4096 + ncol]
                cmp_ = B32[:, 8192:10240]
                ldma(buf, w_t_fp32[rt * 128:(rt + 1) * 128, :], writes=(bufk,))
                V_(lambda e, b=buf, sc=scale_col: e.tensor_scalar(
                    out=b, in0=b, scalar1=sc, scalar2=None, op0=ALU.mult),
                   reads=(bufk, scol_key), writes=(bufk,))
                nh = max(1, ncol // 2048)
                hw = min(2048, ncol)
                for hh in range(nh):
                    bh = buf[:, hh * hw:(hh + 1) * hw]
                    cm = cmp_[:, 0:hw]
                    V_(lambda e, b=bh, c=cm: e.tensor_scalar(
                        out=c, in0=b, scalar1=0.5, scalar2=None,
                        op0=ALU.is_gt),
                       reads=(bufk,), writes=("Bc",))
                    V_(lambda e, b=bh, c=cm: e.scalar_tensor_tensor(
                        out=c, in0=b, scalar=-0.5, in1=c,
                        op0=ALU.is_ge, op1=ALU.add),
                       reads=(bufk, "Bc"), writes=("Bc",))
                    if dst_sb is not None:
                        dd = dst_sb[:, rt * ncol + hh * hw:
                                    rt * ncol + (hh + 1) * hw]
                        dk = dst_key + "_t"
                    else:
                        dd = S2[:, hh * hw:(hh + 1) * hw]
                        dk = dst_key + "_s"
                    V_(lambda e, d=dd, c=cm: e.tensor_scalar(
                        out=d, in0=c, scalar1=-1.0, scalar2=None,
                        op0=ALU.add),
                       reads=("Bc",), writes=(dk,))
                if dst_dram is not None:
                    src_sb = (dst_sb[:, rt * ncol:(rt + 1) * ncol]
                              if dst_sb is not None else S2[:, 0:ncol])
                    rk = dst_key + ("_t" if dst_sb is not None else "_s")
                    ldma(dst_dram[rt * 128:(rt + 1) * 128, :], src_sb,
                         reads=(rk,), writes=(dst_key + "_dram",))

        A_gate = A[:, 0:32768]
        A_up = A[:, 32768:65536]
        ternarize(wg_t, colA[:, 0:1], "colA_sg", A_gate, "terng",
                  tern_g_d, 8, 4096)
        ternarize(wu_t, colA[:, 1:2], "colA_su", A_up, "ternu",
                  tern_u_d, 8, 4096)
        ternarize(wd_t, colA[:, 2:3], "colA_sd", None, "ternd",
                  tern_d_d, 32, 1024)

        class Ctx:
            def __init__(self):
                self.l = []
            def T(self, ems, reads=(), writes=()):
                self.l.append(("tensor", ems, reads, writes))
            def V(self, em, reads=(), writes=()):
                self.l.append(("vector", [em], reads, writes))
            def SC(self, em, reads=(), writes=()):
                self.l.append(("scalar", [em], reads, writes))
            def DMA(self, dst, src, reads=(), writes=()):
                self.l.append(("dma",
                               [lambda e, d=dst, s=src:
                                e.dma_start(out=d, in_=s)], reads, writes))

        class Direct:
            def T(self, ems, reads=(), writes=()):
                tp.group("tensor", ems, reads, writes)
            def V(self, em, reads=(), writes=()):
                tp.op("vector", em, reads, writes)
            def SC(self, em, reads=(), writes=()):
                tp.op("scalar", em, reads, writes)
            def DMA(self, dst, src, reads=(), writes=()):
                ldma(dst, src, reads=reads, writes=writes)

        direct = Direct()

        def emit_ctx(lst):
            for eng, ems, rd, wr in lst:
                if eng == "dma":
                    tp.op("sync", ems[0], rd, wr, dma=True)
                else:
                    tp.group(eng, ems, rd, wr)

        def merge2(a, bq):
            out = []
            for i in range(max(len(a), len(bq))):
                if i < len(a):
                    out.append(a[i])
                if i < len(bq):
                    out.append(bq[i])
            return out

        def layer_norm(rt, rtk, par, E=None, xl=None, xlk=None, sqpb=None,
                       first_extra=()):
            if E is None:
                E = direct
            if xl is None:
                xl, xlk = xln[:], "xln"
            cb = par * 32
            c = lambda j: colB[:, cb + j:cb + j + 1]
            ck = lambda j: f"colB{par}_{j}"
            E.V(lambda e: e.reduce_sum(out=c(0), in_=rt[:], axis=AX.X),
                reads=(rtk,), writes=(ck(0),))
            E.V(lambda e: e.tensor_scalar(out=c(1), in0=c(0),
                                          scalar1=float(-1.0 / D),
                                          scalar2=None, op0=ALU.mult),
                reads=(ck(0),), writes=(ck(1),))
            E.V(lambda e, xl=xl: e.tensor_scalar(out=xl, in0=rt[:],
                                                 scalar1=c(1),
                                                 scalar2=None, op0=ALU.add),
                reads=(rtk, ck(1)), writes=(xlk,) + tuple(first_extra))
            if sqpb is None:
                E.SC(lambda e, xl=xl: e.activation(out=zk[:], in_=xl,
                                                   func=AF.Square,
                                                   accum_out=c(2)),
                     reads=(xlk,), writes=("zk_s0", "zk_s1", ck(2)))
            else:
                pbank, pbk = sqpb
                E.SC(lambda e, xl=xl: e.activation(
                    out=pbank[:, 0:512], in_=xl[:, 0:512], func=AF.Square,
                    accum_out=c(24)),
                     reads=(xlk,), writes=(pbk, ck(24)))
                E.SC(lambda e, xl=xl: e.activation(
                    out=pbank[:, 0:512], in_=xl[:, 512:1024], func=AF.Square,
                    accum_out=c(25)),
                     reads=(xlk,), writes=(pbk, ck(25)))
                E.V(lambda e: e.tensor_tensor(out=c(2), in0=c(24), in1=c(25),
                                              op=ALU.add),
                    reads=(ck(24), ck(25)), writes=(ck(2),))
            E.V(lambda e: e.tensor_scalar(out=c(3), in0=c(2),
                                          scalar1=float(1.0 / D), scalar2=1e-5,
                                          op0=ALU.mult, op1=ALU.add),
                reads=(ck(2),), writes=(ck(3),))
            E.SC(lambda e: e.activation(out=c(4), in_=c(3), func=AF.Ln),
                 reads=(ck(3),), writes=(ck(4),))
            E.SC(lambda e: e.activation(out=c(5), in_=c(4), func=AF.Exp,
                                        scale=-0.5),
                 reads=(ck(4),), writes=(ck(5),))
            E.V(lambda e, xl=xl: e.tensor_scalar(out=xl, in0=xl, scalar1=c(5),
                                                 scalar2=None, op0=ALU.mult),
                reads=(xlk, ck(5)), writes=(xlk,))
            return c, ck

        def ln_and_quant(tile, rt, rtk, par, E):
            if par == 0:
                xl, xlk = xln[:], "xln"
                zsl = lambda hh: (zk[:, hh * 512:(hh + 1) * 512],
                                  f"zk_s{hh}")
            else:
                xl, xlk = S232[:, 0:1024], "xln1"
                zsl = lambda hh: (S232[:, 1024 + hh * 512:
                                       1024 + (hh + 1) * 512], f"z1_{hh}")
            c, ck = layer_norm(rt, rtk, par, E=E, xl=xl, xlk=xlk,
                               sqpb=(pb[6 + par], f"pb{6 + par}"),
                               first_extra=(("ternd_s",) if par == 1 else ()))
            E.V(lambda e, xl=xl: e.reduce_max(out=c(6), in_=xl, axis=AX.X,
                                              apply_absolute_value=True),
                reads=(xlk,), writes=(ck(6),))
            E.V(lambda e: e.tensor_scalar(out=c(7), in0=c(6), scalar1=1e-5,
                                          scalar2=None, op0=ALU.max),
                reads=(ck(6),), writes=(ck(7),))
            E.V(lambda e: e.reciprocal(out=c(8), in_=c(7)),
                reads=(ck(7),), writes=(ck(8),))
            E.V(lambda e: e.tensor_scalar(out=c(9), in0=c(8), scalar1=127.0,
                                          scalar2=None, op0=ALU.mult),
                reads=(ck(8),), writes=(ck(9),))
            E.V(lambda e: e.reciprocal(out=c(10), in_=c(9)),
                reads=(ck(9),), writes=(ck(10),))
            E.V(lambda e: e.tensor_tensor(out=c(11), in0=c(10),
                                          in1=colA[:, 3:4], op=ALU.mult),
                reads=(ck(10), "colA_cg"), writes=(ck(11),))
            E.V(lambda e: e.tensor_tensor(out=c(12), in0=c(10),
                                          in1=colA[:, 4:5], op=ALU.mult),
                reads=(ck(10), "colA_cu"), writes=(ck(12),))
            xqb = xqb0 if par == 0 else xqb1
            for hh in range(2):
                zs, zsk = zsl(hh)
                E.V(lambda e, z=zs, xl=xl, hh=hh: e.tensor_scalar(
                    out=z, in0=xl[:, hh * 512:(hh + 1) * 512], scalar1=c(9),
                    scalar2=0.5, op0=ALU.mult, op1=ALU.add),
                    reads=(xlk, ck(9)), writes=(zsk,))
                E.SC(lambda e, z=zs: e.copy(out=qi[:], in_=z),
                     reads=(zsk,), writes=("qi_s",))
                E.V(lambda e, z=zs: e.scalar_tensor_tensor(
                    out=z, in0=qi[:], scalar=1.0, in1=z,
                    op0=ALU.mult, op1=ALU.is_gt),
                    reads=("qi_s", zsk), writes=(zsk,))
                E.V(lambda e, z=zs, d=xqb, hh=hh: e.tensor_tensor(
                    out=d[:, hh * 512:(hh + 1) * 512], in0=qi[:], in1=z,
                    op=ALU.subtract),
                    reads=("qi_s", zsk), writes=(f"xqb{par}",))
            return c, ck

        def transpose_128(src_ap, srck, dst_ap, dstk, nblk, pbank, pbkey,
                          E=None):
            if E is None:
                E = direct
            for g0 in range(0, nblk, 4):
                gn = min(4, nblk - g0)
                ems = []
                for j in range(gn):
                    ems.append(MM(pbank[:, j * 128:(j + 1) * 128],
                                  src_ap[:, (g0 + j) * 128:(g0 + j + 1) * 128],
                                  identb[:], True, True))
                E.T(ems, reads=(srck, "identb"), writes=(pbkey,))
                E.SC(lambda e, g0=g0, gn=gn: e.copy(
                    out=dst_ap[:, g0 * 128:(g0 + gn) * 128],
                    in_=pbank[:, 0:gn * 128]),
                     reads=(pbkey,), writes=(dstk,))

        def sweep1_partA(i, src_dram):
            E = Ctx()
            par = i % 2
            rt = rt0 if par == 0 else rt1
            rtk = f"rt{par}"
            E.DMA(rt[:], src_dram[i * 128:(i + 1) * 128, :],
                  reads=(f"res{i}",), writes=(rtk,))
            c, ck = ln_and_quant(i, rt, rtk, par, E)
            xqb = xqb0 if par == 0 else xqb1
            xqf = xqf0 if par == 0 else xqf1
            transpose_128(xqb[:], f"xqb{par}", xqf[:], f"xqf{par}", 8,
                          pb[4 + par][:, 0:512], f"pb{4 + par}", E=E)
            return E.l, c, ck

        def sweep1_partB(i, c, ck):
            E = Ctx()
            par = i % 2
            xqf = xqf0 if par == 0 else xqf1
            hbase = par * 4096
            eg_ap, eu_ap = c(11), c(12)
            if par == 0:
                zsl = lambda s: (zk[:, (s % 2) * 512:(s % 2) * 512 + 512],
                                 f"zk_s{s % 2}")
            else:
                zsl = lambda s: (S232[:, 1024 + (s % 2) * 512:
                                      1024 + (s % 2) * 512 + 512],
                                 f"z1_{s % 2}")
            for ic in range(8):
                bk = ic % 2
                hsl = B32[:, hbase + ic * 512:hbase + (ic + 1) * 512]
                ems = [MM(pb[bk][:, 0:512],
                          xqf[:, fc * 128:(fc + 1) * 128],
                          A_gate[:, fc * 4096 + ic * 512:
                                 fc * 4096 + (ic + 1) * 512],
                          fc == 0, fc == 7) for fc in range(8)]
                E.T(ems, reads=(f"xqf{par}", "terng_t"), writes=(f"pb{bk}",))
                E.SC(lambda e, h=hsl, b=bk, s=eg_ap: e.activation(
                    out=h, in_=pb[b][:, 0:512], func=AF.Silu, scale=s),
                     reads=(f"pb{bk}", ck(11)),
                     writes=(f"h{par}_{ic}",))
                ems = [MM(pb[2 + bk][:, 0:512],
                          xqf[:, fc * 128:(fc + 1) * 128],
                          A_up[:, fc * 4096 + ic * 512:
                               fc * 4096 + (ic + 1) * 512],
                          fc == 0, fc == 7) for fc in range(8)]
                E.T(ems, reads=(f"xqf{par}", "ternu_t"),
                    writes=(f"pb{2 + bk}",))
                E.V(lambda e, h=hsl, b=bk, s=eu_ap: e.scalar_tensor_tensor(
                    out=h, in0=pb[2 + b][:, 0:512], scalar=s, in1=h,
                    op0=ALU.mult, op1=ALU.mult),
                    reads=(f"pb{2 + bk}", ck(12), f"h{par}_{ic}"),
                    writes=(f"h{par}_{ic}",))
            E.V(lambda e, h=B32[:, hbase:hbase + 2048], o=c(22): e.reduce_max(
                out=o, in_=h, axis=AX.X, apply_absolute_value=True),
                reads=tuple(f"h{par}_{ic}" for ic in range(4)),
                writes=(ck(22),))
            E.V(lambda e, h=B32[:, hbase + 2048:hbase + 4096],
                o=c(23): e.reduce_max(
                out=o, in_=h, axis=AX.X, apply_absolute_value=True),
                reads=tuple(f"h{par}_{ic}" for ic in range(4, 8)),
                writes=(ck(23),))
            E.V(lambda e, a=c(22), b2=c(23), o=c(16): e.tensor_tensor(
                out=o, in0=a, in1=b2, op=ALU.max),
                reads=(ck(22), ck(23)), writes=(ck(16),))
            E.V(lambda e, a=c(16), o=c(17): e.tensor_scalar(
                out=o, in0=a, scalar1=1e-5, scalar2=None, op0=ALU.max),
                reads=(ck(16),), writes=(ck(17),))
            E.V(lambda e, a=c(17), o=c(18): e.reciprocal(out=o, in_=a),
                reads=(ck(17),), writes=(ck(18),))
            E.V(lambda e, a=c(18), o=c(19): e.tensor_scalar(
                out=o, in0=a, scalar1=127.0, scalar2=None, op0=ALU.mult),
                reads=(ck(18),), writes=(ck(19),))
            E.V(lambda e, a=c(19), o=c(20): e.reciprocal(out=o, in_=a),
                reads=(ck(19),), writes=(ck(20),))
            E.V(lambda e, a=c(20), o=c(21): e.tensor_tensor(
                out=o, in0=a, in1=colA[:, 5:6], op=ALU.mult),
                reads=(ck(20), "colA_cd"), writes=(ck(21),))
            E.V(lambda e, a=c(21), i=i: e.tensor_scalar(
                out=hsc[:, i:i + 1], in0=a, scalar1=0.5, scalar2=None,
                op0=ALU.mult),
                reads=(ck(21),), writes=(f"hsc{i}",))
            hq_sb = B[:, 16384:20480]
            hscale_ap = c(19)
            for s in range(8):
                sl = slice(s * 512, (s + 1) * 512)
                hk = f"h{par}_{s}"
                hslice = B32[:, hbase + sl.start:hbase + sl.stop]
                zs, zsk = zsl(s)
                E.V(lambda e, h=hslice, z=zs, sc=hscale_ap: e.tensor_scalar(
                    out=z, in0=h, scalar1=sc, scalar2=0.5,
                    op0=ALU.mult, op1=ALU.add),
                    reads=(hk, ck(19)), writes=(zsk,))
                E.SC(lambda e, z=zs: e.copy(out=qi[:], in_=z),
                     reads=(zsk,), writes=("qi_s",))
                E.V(lambda e, z=zs: e.scalar_tensor_tensor(
                    out=z, in0=qi[:], scalar=1.0, in1=z, op0=ALU.mult,
                    op1=ALU.is_gt),
                    reads=("qi_s", zsk), writes=(zsk,))
                E.V(lambda e, z=zs, sl=sl: e.tensor_tensor(
                    out=hq_sb[:, sl], in0=qi[:], in1=z,
                    op=ALU.subtract),
                    reads=("qi_s", zsk), writes=(f"hq_{s}",))
            E.DMA(hq_d[i * 128:(i + 1) * 128, :], hq_sb,
                  reads=tuple(f"hq_{s}" for s in range(8)),
                  writes=("hq_dram",))
            return E.l

        def ffn_sweep1(src_dram, phase):
            prevB = None
            for i in range(NT):
                la, c, ck = sweep1_partA(i, src_dram)
                if prevB is None:
                    emit_ctx(la)
                else:
                    emit_ctx(merge2(prevB, la))
                prevB = sweep1_partB(i, c, ck)
            emit_ctx(prevB)

        def sweep2_partA(i, res_src):
            E = Ctx()
            par = i % 2
            hqv = B[:, par * 4096:(par + 1) * 4096]
            hqk = f"s2hq{par}"
            hqf = B[:, 8192 + par * 4096:8192 + (par + 1) * 4096]
            hqfk = f"s2hqf{par}"
            E.DMA(hqv, hq_d[i * 128:(i + 1) * 128, :],
                  reads=("hq_dram",), writes=(hqk,))
            transpose_128(hqv, hqk, hqf, hqfk, 32,
                          pb[4 + par][:, 0:512], f"pb{4 + par}", E=E)
            rt = rt0 if par == 0 else rt1
            rtk = f"rt{par}"
            E.DMA(rt[:], res_src[i * 128:(i + 1) * 128, :],
                  reads=(f"res{i}",), writes=(rtk,))
            return E.l

        def sweep2_partB(i, dst_dram):
            E = Ctx()
            par = i % 2
            hqf = B[:, 8192 + par * 4096:8192 + (par + 1) * 4096]
            hqfk = f"s2hqf{par}"
            rt = rt0 if par == 0 else rt1
            rtk = f"rt{par}"
            for h2 in range(2):
                bk = par * 2 + h2
                ems = [MM(pb[bk][:, 0:512],
                          hqf[:, ic * 128:(ic + 1) * 128],
                          A[:, ic * 1024 + h2 * 512:
                            ic * 1024 + h2 * 512 + 512],
                          ic == 0, ic == 31) for ic in range(32)]
                E.T(ems, reads=(hqfk, "Adown"), writes=(f"pb{bk}",))
                E.V(lambda e, h2=h2, bk=bk, i=i, r=rt: e.scalar_tensor_tensor(
                    out=r[:, h2 * 512:(h2 + 1) * 512],
                    in0=pb[bk][:, 0:512], scalar=hsc[:, i:i + 1],
                    in1=r[:, h2 * 512:(h2 + 1) * 512],
                    op0=ALU.mult, op1=ALU.add),
                    reads=(f"pb{bk}", f"hsc{i}", rtk), writes=(rtk,))
            E.DMA(dst_dram[i * 128:(i + 1) * 128, :], rt[:],
                  reads=(rtk,), writes=(f"res{i}",))
            return E.l

        def ffn_sweep2(res_src, dst_dram):
            ldma(A[:, 0:32768],
                 tern_d_d.rearrange("(a p) c -> p a c", a=32, p=128),
                 reads=("ternd_dram",), writes=("Adown",))
            prevB = None
            for i in range(NT):
                la = sweep2_partA(i, res_src)
                if prevB is None:
                    emit_ctx(la)
                else:
                    emit_ctx(merge2(prevB, la))
                prevB = sweep2_partB(i, dst_dram)
            emit_ctx(prevB)

        def build_attn():
            NB = T // 256
            QTf = lambda h, cc: B[:, h * 256 + cc * 128:h * 256 + cc * 128 + 128]
            Kf_ = lambda h, cc: B[:, 2048 + h * 256 + cc * 128:
                                  2048 + h * 256 + cc * 128 + 128]
            BKt = lambda h, cc: B[:, 4096 + (cc * 8 + h) * 128:
                                  4096 + (cc * 8 + h) * 128 + 128]
            Kh_ = lambda h, cc: B[:, 6144 + (cc * 8 + h) * 128:
                                  6144 + (cc * 8 + h) * 128 + 128]
            Vb_ = lambda h, cc: B[:, 8192 + (cc * 8 + h) * 256:
                                  8192 + (cc * 8 + h) * 256 + 256]
            GVs = lambda cc, h: B[:, 12288 + h * 256:12288 + h * 256 + 256]
            xlnfm = lambda fc: B[:, 16384 + fc * 256:16384 + fc * 256 + 256]
            xlnb = B[:, 14336:15360]
            Sbf = lambda h: B[:, 18432 + h * 256:18432 + (h + 1) * 256]
            Sst = lambda h: S132[:, h * 256:(h + 1) * 256]
            G32 = S232[:, 0:128]
            Dls = S232[:, 128:256]
            D2_ = S232[:, 256:384]
            DA_ = S232[:, 384:512]
            Nt_ = S2[:, 1024:1152]
            NTt = S2[:, 1152:1280]
            Aqk = S2[:, 1280:1408]
            Pw = lambda j: S2[:, 1408 + j * 128:1408 + (j + 1) * 128]
            sqd = S232[:, 704:960]
            qsb = S232[:, 960:1216]
            ogt = S2[:, 2432:2560]
            DAb = S2[:, 2560:2688]
            Yc = lambda j: S2[:, 2688 + j * 128:2688 + (j + 1) * 128]
            Vp_ = S232[:, 1472:1728]
            WT_ = S2[:, 3456:3584]
            DAt = S2[:, 3712:3840]
            ogf = S2[:, 3712:3840]
            U_ = S2[:, 3840:4096]
            og2 = S2[:, 3840:4096]

            def wload(dst_off, w_d_, nblk, blkcols):
                for a_ in range(nblk):
                    par = a_ % 2
                    buf = B32[:, par * 2048:par * 2048 + blkcols]
                    ldma(buf, w_d_[a_ * 128:(a_ + 1) * 128, :],
                         writes=(f"wldB{par}",))
                    V_(lambda e, b=buf, a_=a_: e.tensor_copy(
                        out=A[:, dst_off + a_ * blkcols:
                              dst_off + (a_ + 1) * blkcols],
                        in_=b),
                       reads=(f"wldB{par}",),
                       writes=("Wq" if dst_off == 0 else
                               "Wk" if dst_off == 8192 else
                               "Wv" if dst_off == 16384 else
                               "Wgv" if dst_off == 32768 else "Wo",
                               f"wldB{par}"))
            wload(0, wq_d, 8, 1024)
            wload(8192, wk_d, 8, 1024)
            wload(16384, wv_d, 8, 2048)
            wload(32768, wgv_d, 8, 2048)
            wload(49152, wo_d, 16, 1024)
            ldma(wba_f[:], wba_d.rearrange("(a p) c -> p a c", a=8, p=128),
                 writes=("wbaf",))
            V_(lambda e: e.tensor_copy(out=wba_sb[:], in_=wba_f[:]),
               reads=("wbaf",), writes=("wba",))
            ldma(convw[:, 0:32],
                 convq_d.rearrange("(a p) c -> p a c", a=8, p=128),
                 writes=("convw",))
            ldma(convw[:, 32:64],
                 convk_d.rearrange("(a p) c -> p a c", a=8, p=128),
                 writes=("convw2",))
            ldma(convw[:, 64:128],
                 convv_d.rearrange("(a p) c -> p a c", a=16, p=128),
                 writes=("convw3",))
            for ic in range(16):
                V_(lambda e, ic=ic: e.tensor_scalar(
                    out=A[:, 49152 + ic * 1024:49152 + (ic + 1) * 1024],
                    in0=A[:, 49152 + ic * 1024:49152 + (ic + 1) * 1024],
                    scalar1=normw_c[:, ic % 2:ic % 2 + 1], scalar2=None,
                    op0=ALU.mult),
                   reads=("Wo", "normw"), writes=("Wo",))
            SC_(lambda e: e.activation(out=alog_c[0:8, 1:2],
                                       in_=alog_c[0:8, 0:1], func=AF.Exp),
                reads=("alog",), writes=("alogE",))
            V_(lambda e: e.tensor_scalar(out=alog_c[0:8, 1:2],
                                         in0=alog_c[0:8, 1:2], scalar1=-1.0,
                                         scalar2=None, op0=ALU.mult),
               reads=("alogE",), writes=("alogE",))
            V_(lambda e: e.memset(S132[:, 0:2048], 0.0),
               writes=tuple(f"S{h}" for h in range(8)))
            V_(lambda e: e.memset(B[:, 18432:20480], 0.0),
               writes=tuple(f"Sbf{h}" for h in range(8)))
            V_(lambda e: e.memset(convtail[:, 0:96], 0.0), writes=("ctail",))

            beta_hc = gch
            LNB = 0.0883883476483184395
            import math
            LN_DKS = math.log(LNB)

            if attn_stage < 0:
                return
            for b in range(NB):
                if attn_stage == 0.1 and b > 0:
                    break
                for cc in range(2):
                    i = 2 * b + cc
                    rt = rt0 if cc == 0 else rt1
                    rtk = f"rt{cc}"
                    ldma(rt[:], res_d[i * 128:(i + 1) * 128, :],
                         reads=(f"res{i}",), writes=(rtk,))
                    c, ck = layer_norm(rt, rtk, cc)
                    V_(lambda e: e.tensor_copy(out=xlnb[:], in_=xln[:]),
                       reads=("xln",), writes=("xlnb",))
                    for g0 in range(0, 8, 4):
                        ems = [MM(pb[2][:, j * 128:(j + 1) * 128],
                                  xlnb[:, (g0 + j) * 128:(g0 + j + 1) * 128],
                                  identb[:], True, True) for j in range(4)]
                        T_(ems, reads=("xlnb", "identb"), writes=("pb2",))
                        for j in range(4):
                            V_(lambda e, fc=g0 + j, j=j, cc=cc: e.tensor_copy(
                                out=xlnfm(fc)[:, cc * 128:(cc + 1) * 128],
                                in_=pb[2][:, j * 128:(j + 1) * 128]),
                               reads=("pb2",), writes=("xlnfm",))
                if attn_stage < 0.3:
                    continue
                ems = [MM(pb[3][0:8, 0:256],
                          wba_sb[:, fc * 16:fc * 16 + 8],
                          xlnfm(fc)[:], fc == 0, fc == 7) for fc in range(8)]
                ems += [MM(pb[3][0:8, 256:512],
                           wba_sb[:, fc * 16 + 8:fc * 16 + 16],
                           xlnfm(fc)[:], fc == 0, fc == 7) for fc in range(8)]
                T_(ems, reads=("xlnfm", "wba"), writes=("pb3",))
                SC_(lambda e: e.activation(out=zk[0:8, 768:1024],
                                           in_=pb[3][0:8, 0:256],
                                           func=AF.Exp, scale=-1.0),
                    reads=("pb3",), writes=("bsg", "zk_s1"))
                SC_(lambda e: e.activation(out=zk[0:8, 768:1024],
                                           in_=zk[0:8, 768:1024],
                                           func=AF.Ln, bias=1.0),
                    reads=("bsg",), writes=("bsg",))
                SC_(lambda e: e.activation(out=zk[0:8, 0:256],
                                           in_=zk[0:8, 768:1024],
                                           func=AF.Exp, scale=-1.0),
                    reads=("bsg",), writes=("beta8",))
                SC_(lambda e: e.activation(out=zk[0:8, 256:512],
                                           in_=pb[3][0:8, 256:512],
                                           func=AF.Exp,
                                           bias=alog_c[0:8, 2:3]),
                    reads=("pb3", "alog2"), writes=("g8e",))
                SC_(lambda e: e.activation(out=zk[0:8, 512:768],
                                           in_=zk[0:8, 256:512],
                                           func=AF.Ln, bias=1.0),
                    reads=("g8e",), writes=("g8l",))
                V_(lambda e: e.tensor_scalar(out=zk[0:8, 512:768],
                                             in0=zk[0:8, 512:768],
                                             scalar1=alog_c[0:8, 1:2],
                                             scalar2=None, op0=ALU.mult),
                   reads=("g8l", "alogE"), writes=("g8",))
                ldma(gch[:], bass.AP(zk, 512,
                                     [[1024, 8], [128, 2], [1, 128]]),
                     reads=("g8",), writes=("gch",))
                ldma(bech[:], bass.AP(zk, 0,
                                      [[1024, 8], [128, 2], [1, 128]]),
                     reads=("beta8",), writes=("bech",))
                V_(lambda e: e.tensor_tensor_scan(out=bch[:], data0=gch[:],
                                                  data1=gch[:], initial=0.0,
                                                  op0=ALU.add, op1=ALU.bypass),
                   reads=("gch",), writes=("bch",))
                V_(lambda e: e.tensor_scalar(out=ehat[:], in0=bch[:],
                                             scalar1=bch[:, 127:128],
                                             scalar2=None, op0=ALU.subtract),
                   reads=("bch",), writes=("ehatp",))
                SC_(lambda e: e.activation(out=ehat[:], in_=ehat[:],
                                           func=AF.Exp, scale=-1.0),
                    reads=("ehatp",), writes=("ehat",))
                for tsrc, tk, tdst, tdk in ((bch, "bch", bT, "bT"),
                                            (bech, "bech", betaT, "betaT"),
                                            (ehat, "ehat", ehatT2, "ehatT")):
                    T_([MM(pb[7][:, 0:16], tsrc[:], identf[0:16, 0:16],
                           True, True)],
                       reads=(tk, "identf"), writes=("pb7",))
                    V_(lambda e, d=tdst: e.tensor_copy(out=d[:],
                                                       in_=pb[7][:, 0:16]),
                       reads=("pb7",), writes=(tdk,))
                V_(lambda e: e.tensor_scalar(out=bnT[:], in0=bT[:],
                                             scalar1=-1.0, scalar2=None,
                                             op0=ALU.mult),
                   reads=("bT",), writes=("bnT",))
                SC_(lambda e: e.activation(out=gamT[:], in_=bT[:], func=AF.Exp),
                    reads=("bT",), writes=("gamT",))
                V_(lambda e: e.tensor_scalar(out=nbetaT[:], in0=betaT[:],
                                             scalar1=-1.0, scalar2=None,
                                             op0=ALU.mult),
                   reads=("betaT",), writes=("nbetaT",))
                V_(lambda e: e.tensor_mul(out=bgT[:], in0=betaT[:],
                                          in1=gamT[:]),
                   reads=("betaT", "gamT"), writes=("bgT",))
                ldma(sc_c[0:1, 16:32], bT[127:128, 0:16],
                     reads=("bT",), writes=("bCrow",))
                T_([MM(pb[7][:, 16:32], onesf[0:1, 0:128],
                       sc_c[0:1, 16:32], True, True)],
                   reads=("bCrow", "onesf"), writes=("pb7",))
                SC_(lambda e: e.activation(out=ebC[:], in_=pb[7][:, 16:32],
                                           func=AF.Exp),
                    reads=("pb7",), writes=("ebC",))

                if attn_stage < 0.5:
                    continue
                def proj_conv(kind, oc, wbase, cwbase):
                    bk = oc % 2
                    ems = [MM(pb[bk][:, 0:256],
                              A[:, wbase + fc * (1024 if kind != "v" else 2048)
                                + oc * 128:
                                wbase + fc * (1024 if kind != "v" else 2048)
                                + (oc + 1) * 128],
                              xlnfm(fc)[:], fc == 0, fc == 7)
                           for fc in range(8)]
                    wk_key = {"q": "Wq", "k": "Wk", "v": "Wv"}[kind]
                    T_(ems, reads=("xlnfm", wk_key), writes=(f"pb{bk}",))
                    ti = {"q": 0, "k": 1, "v": 2}[kind]
                    tail = convtail[:, (cwbase + oc) * 3:(cwbase + oc + 1) * 3]
                    SC_(lambda e: e.copy(out=xtr[:, 3:259],
                                         in_=pb[bk][:, 0:256]),
                        reads=(f"pb{bk}",), writes=("xtr",))
                    V_(lambda e, t=tail: e.tensor_copy(out=xtr[:, 0:3], in_=t),
                       reads=("ctail",), writes=("xtr3",))
                    V_(lambda e, t=tail: e.tensor_copy(out=t,
                                                       in_=xtr[:, 256:259]),
                       reads=("xtr",), writes=("ctail",))
                    cw = lambda tap: convw[:, (cwbase + oc) * 4 + tap:
                                           (cwbase + oc) * 4 + tap + 1]
                    eng = V_
                    eng(lambda e: e.tensor_scalar(
                        out=zk[:, 0:256], in0=xtr[:, 0:256], scalar1=cw(0),
                        scalar2=None, op0=ALU.mult),
                        reads=("xtr", "xtr3", "convw"),
                        writes=("cacc", "zk_s0", "beta8", "g8e"))
                    for tap in range(1, 4):
                        eng(lambda e, tap=tap: e.scalar_tensor_tensor(
                            out=zk[:, 0:256], in0=xtr[:, tap:tap + 256],
                            scalar=cw(tap), in1=zk[:, 0:256],
                            op0=ALU.mult, op1=ALU.add),
                            reads=("xtr", "xtr3", "convw", "cacc"),
                            writes=("cacc",))
                    zkb = zk[:].bitcast(BF16)
                    SC_(lambda e: e.activation(out=zk[:, 768:1024],
                                               in_=zk[:, 0:256],
                                               func=AF.Exp, scale=-1.0),
                        reads=("cacc",), writes=("bsg", "zk_s1"))
                    SC_(lambda e: e.activation(out=zk[:, 768:1024],
                                               in_=zk[:, 768:1024],
                                               func=AF.Ln, bias=1.0),
                        reads=("bsg",), writes=("bsg",))
                    SC_(lambda e: e.activation(out=zk[:, 768:1024],
                                               in_=zk[:, 768:1024],
                                               func=AF.Exp, scale=-1.0),
                        reads=("bsg",), writes=("bsg",))
                    V_(lambda e: e.tensor_tensor(out=zkb[:, 1024:1280],
                                                 in0=zk[:, 0:256],
                                                 in1=zk[:, 768:1024],
                                                 op=ALU.mult),
                       reads=("cacc", "bsg"),
                       writes=("sil", "zk_s1", "g8l", "g8"))
                    return zkb[:, 1024:1280]

                for h in range(H):
                    if attn_stage < 0.52 + 0.001 * h:
                        continue
                    sil = proj_conv("q", h, 0, 0)
                    if attn_stage < 0.54:
                        continue
                    for cc in range(2):
                        hc = h * 2 + cc
                        T_([MM(pb[2][:, 0:128], sil[:, cc * 128:(cc + 1) * 128],
                               identb[:], True, True)],
                           reads=("sil", "identb"), writes=("pb2",))
                        V_(lambda e: e.tensor_copy(out=ogt[:, 0:128],
                                                   in_=pb[2][:, 0:128]),
                           reads=("pb2",), writes=("P5",))
                        SC_(lambda e: e.activation(
                            out=sqd[:, 0:128], in_=ogt[:, 0:128],
                            func=AF.Square, accum_out=colB[:, 40:41]),
                            reads=("P5",), writes=("P1", "P1T", "ssq"))
                        SC_(lambda e: e.activation(out=colB[:, 41:42],
                                                   in_=colB[:, 40:41],
                                                   func=AF.Ln, bias=1e-6),
                            reads=("ssq",), writes=("ssl",))
                        SC_(lambda e: e.activation(out=colB[:, 42:43],
                                                   in_=colB[:, 41:42],
                                                   func=AF.Exp, scale=-0.5,
                                                   bias=LN_DKS),
                            reads=("ssl",), writes=("rstdq",))
                        V_(lambda e, hc=hc, h=h, cc=cc: e.tensor_scalar(
                            out=ogt[:, 0:128], in0=ogt[:, 0:128],
                            scalar1=colB[:, 42:43], scalar2=None,
                            op0=ALU.mult),
                           reads=("P5", "rstdq"), writes=("P5",))
                        T_([MM(pb[2][:, 128:256], ogt[:, 0:128], identb[:],
                               True, True)],
                           reads=("P5", "identb"), writes=("pb2",))
                        V_(lambda e, h=h, cc=cc: e.tensor_copy(
                            out=QTf(h, cc), in_=pb[2][:, 128:256]),
                           reads=("pb2",), writes=("QTf",))
                    if attn_stage < 0.56:
                        continue
                    sil = proj_conv("k", h, 8192, 8)
                    for cc in range(2):
                        hc = h * 2 + cc
                        T_([MM(pb[2][:, 0:128], sil[:, cc * 128:(cc + 1) * 128],
                               identb[:], True, True)],
                           reads=("sil", "identb"), writes=("pb2",))
                        V_(lambda e: e.tensor_copy(out=ogt[:, 0:128],
                                                   in_=pb[2][:, 0:128]),
                           reads=("pb2",), writes=("P5",))
                        SC_(lambda e: e.activation(
                            out=sqd[:, 0:128], in_=ogt[:, 0:128],
                            func=AF.Square, accum_out=colB[:, 43:44]),
                            reads=("P5",), writes=("P1", "P1T", "ssk"))
                        SC_(lambda e: e.activation(out=colB[:, 44:45],
                                                   in_=colB[:, 43:44],
                                                   func=AF.Ln, bias=1e-6),
                            reads=("ssk",), writes=("sskl",))
                        SC_(lambda e: e.activation(out=colB[:, 45:46],
                                                   in_=colB[:, 44:45],
                                                   func=AF.Exp, scale=-0.5),
                            reads=("sskl",), writes=("rstdk",))
                        V_(lambda e, hc=hc: e.tensor_tensor(
                            out=colB[:, 46:47], in0=colB[:, 45:46],
                            in1=bgT[:, hc:hc + 1], op=ALU.mult),
                           reads=("rstdk", "bgT"), writes=("cbk",))
                        V_(lambda e, hc=hc: e.tensor_tensor(
                            out=colB[:, 47:48], in0=colB[:, 45:46],
                            in1=ehatT2[:, hc:hc + 1], op=ALU.mult),
                           reads=("rstdk", "ehatT"), writes=("ckh",))
                        V_(lambda e, h=h, cc=cc: e.tensor_scalar(
                            out=BKt(h, cc), in0=ogt[:, 0:128],
                            scalar1=colB[:, 46:47], scalar2=None,
                            op0=ALU.mult),
                           reads=("P5", "cbk"), writes=("BKt",))
                        V_(lambda e, h=h, cc=cc: e.tensor_scalar(
                            out=Kh_(h, cc), in0=ogt[:, 0:128],
                            scalar1=colB[:, 47:48], scalar2=None,
                            op0=ALU.mult),
                           reads=("P5", "ckh"), writes=("Kh",))
                        V_(lambda e: e.tensor_scalar(
                            out=ogt[:, 0:128], in0=ogt[:, 0:128],
                            scalar1=colB[:, 45:46], scalar2=None,
                            op0=ALU.mult),
                           reads=("P5", "rstdk"), writes=("P5",))
                        T_([MM(pb[2][:, 128:256], ogt[:, 0:128], identb[:],
                               True, True)],
                           reads=("P5", "identb"), writes=("pb2",))
                        V_(lambda e, h=h, cc=cc: e.tensor_copy(
                            out=Kf_(h, cc), in_=pb[2][:, 128:256]),
                           reads=("pb2",), writes=("Kf",))
                    if attn_stage < 0.58:
                        continue
                    for dv in range(2):
                        sil = proj_conv("v", h * 2 + dv, 16384, 16)
                        for cc in range(2):
                            T_([MM(pb[2][:, 0:128],
                                   sil[:, cc * 128:(cc + 1) * 128],
                                   identb[:], True, True)],
                               reads=("sil", "identb"), writes=("pb2",))
                            V_(lambda e, h=h, cc=cc, dv=dv: e.tensor_copy(
                                out=Vb_(h, cc)[:, dv * 128:(dv + 1) * 128],
                                in_=pb[2][:, 0:128]),
                               reads=("pb2",), writes=("Vb",))
                    for cc in range(2):
                        hc = h * 2 + cc
                        V_(lambda e, h=h, cc=cc, hc=hc: e.tensor_scalar(
                            out=Vb_(h, cc), in0=Vb_(h, cc),
                            scalar1=betaT[:, hc:hc + 1], scalar2=None,
                            op0=ALU.mult),
                           reads=("Vb", "betaT"), writes=("Vb",))

                if attn_stage < 0.7:
                    continue
                for cc in range(2):
                    gidx = 2 * b + cc
                    rt = rt0 if cc == 0 else rt1
                    rtk = f"rt{cc}"
                    gvE = Ctx()
                    for qq in range(4):
                        bk = qq % 2
                        ems = [MM(pb[bk][:, 0:512],
                                  xlnfm(fc)[:, cc * 128:(cc + 1) * 128],
                                  A[:, 32768 + fc * 2048 + qq * 512:
                                    32768 + fc * 2048 + (qq + 1) * 512],
                                  fc == 0, fc == 7) for fc in range(8)]
                        gvE.T(ems, reads=("xlnfm", "Wgv"), writes=(f"pb{bk}",))
                        sgv = B32[:, 7680:8192]
                        gvE.SC(lambda e, bk=bk: e.activation(
                            out=sgv, in_=pb[bk][:, 0:512],
                            func=AF.Exp, scale=-1.0),
                            reads=(f"pb{bk}",), writes=("sgv",))
                        gvE.SC(lambda e: e.activation(out=sgv, in_=sgv,
                                                      func=AF.Ln, bias=1.0),
                                reads=("sgv",), writes=("sgv",))
                        gvE.SC(lambda e: e.activation(out=sgv, in_=sgv,
                                                      func=AF.Exp, scale=-1.0),
                                reads=("sgv",), writes=("sgv",))
                        gvE.V(lambda e, qq=qq, bk=bk: e.tensor_tensor(
                            out=B[:, 12288 + qq * 512:12288 + (qq + 1) * 512],
                            in0=pb[bk][:, 0:512], in1=sgv, op=ALU.mult),
                            reads=(f"pb{bk}", "sgv"), writes=("GV",))
                    qiB = qi[:].bitcast(BF16)
                    qi32 = qi[:].bitcast(F32)
                    yapI = lambda j: qiB[:, j * 128:(j + 1) * 128]
                    AqkI = lambda j: qiB[:, 256 + j * 128:256 + (j + 1) * 128]
                    VpS = qi32[:, 256:512]
                    qsb2 = S232[:, 1472:1728]
                    ogf2 = S2[:, 3584:3712]

                    def ph1_ops(h, cc=cc):
                        r = []
                        hc = h * 2 + cc
                        j = h % 2
                        def rT(ems, reads=(), writes=()):
                            r.append(("tensor", ems, reads, writes))
                        def rV(em, reads=(), writes=()):
                            r.append(("vector", [em], reads, writes))
                        def rS(em, reads=(), writes=()):
                            r.append(("scalar", [em], reads, writes))
                        cbc = S232[:, 704:832]
                        rV(lambda e, hc=hc: e.tensor_scalar(
                            out=cbc, in0=onesf128[:], scalar1=bnT[:, hc:hc + 1],
                            scalar2=None, op0=ALU.mult),
                           reads=("onesf128", "bnT"), writes=("P1", "P1T"))
                        rT([MM(pb[3][:, 0:128], cbc, identf[:], True, True)],
                           reads=("P1", "P1T", "identf"), writes=("pb3",))
                        rV(lambda e, hc=hc: e.scalar_tensor_tensor(
                            out=Dls[:], in0=pb[3][:, 0:128],
                            scalar=bT[:, hc:hc + 1], in1=mask_ksl[:],
                            op0=ALU.add, op1=ALU.add),
                           reads=("pb3", "bT", "mask_ksl"), writes=("DlsP",))
                        rS(lambda e: e.activation(out=Dls[:], in_=Dls[:],
                                                  func=AF.Exp),
                           reads=("DlsP",), writes=("Dls",))
                        rV(lambda e: e.scalar_tensor_tensor(
                            out=DAb[:], in0=Dls[:], scalar=1.0,
                            in1=identf[:], op0=ALU.mult, op1=ALU.add),
                           reads=("Dls", "identf"), writes=("P5T",))
                        rT([MM(pb[3][:, 128:256], DAb[:], identb[:],
                               True, True)],
                           reads=("P5T", "identb"), writes=("pb3",))
                        rS(lambda e: e.copy(out=DAt[:], in_=pb[3][:, 128:256]),
                           reads=("pb3",), writes=("DAt",))
                        rT([MM(pb[3][:, 256:384], Kf_(h, cc), Kf_(h, cc),
                               True, True)],
                           reads=("Kf",), writes=("pb3",))
                        rV(lambda e: e.tensor_copy(out=G32[:],
                                                   in_=pb[3][:, 256:384]),
                           reads=("pb3",), writes=("G32",))
                        rT([MM(pb[3][:, 384:512], Kf_(h, cc), QTf(h, cc),
                               True, True)],
                           reads=("Kf", "QTf"), writes=("pb3",))
                        rV(lambda e, j=j: e.scalar_tensor_tensor(
                            out=AqkI(j), in0=pb[3][:, 384:512], scalar=1.0,
                            in1=DAt[:], op0=ALU.mult, op1=ALU.mult),
                           reads=("pb3", "DAt"), writes=(f"AqkI{j}",))
                        rV(lambda e, hc=hc: e.scalar_tensor_tensor(
                            out=Nt_[:], in0=G32[:],
                            scalar=nbetaT[:, hc:hc + 1], in1=Dls[:],
                            op0=ALU.mult, op1=ALU.mult),
                           reads=("G32", "nbetaT", "Dls"), writes=("N",))
                        rT([MM(pb[4][:, 0:128], Nt_[:], identb[:],
                               True, True)],
                           reads=("N", "identb"), writes=("pb4",))
                        rS(lambda e: e.copy(out=NTt[:], in_=pb[4][:, 0:128]),
                           reads=("pb4",), writes=("NT",))
                        def SQ2(lhs1, lk1, rhs1, rk1, lhs2, lk2, rhs2, rk2,
                                slot, k1, k2):
                            rT([MM(pb[4][:, 128:256], lhs1, rhs1, True, True),
                                MM(pb[4][:, 256:384], lhs2, rhs2, True, True)],
                               reads=(lk1, rk1, lk2, rk2), writes=("pb4",))
                            rS(lambda e, s=slot: e.copy(
                                out=S2[:, 1408 + s * 128:1408 + (s + 2) * 128],
                                in_=pb[4][:, 128:384]),
                               reads=("pb4",), writes=(k1, k2))
                        rV(lambda e: e.tensor_tensor(
                            out=Yc(0), in0=identb[:], in1=NTt[:], op=ALU.add),
                           reads=("NT", "identb"), writes=("Y0",))
                        sq_args = [
                            (NTt[:], "NT", Nt_[:], "N",
                             Nt_[:], "N", NTt[:], "NT", 0, "P1", "P1T"),
                            (Pw(1), "P1T", Pw(0), "P1",
                             Pw(0), "P1", Pw(1), "P1T", 2, "P2", "P2T"),
                            (Pw(3), "P2T", Pw(2), "P2",
                             Pw(2), "P2", Pw(3), "P2T", 4, "P3", "P3T"),
                            (Pw(5), "P3T", Pw(4), "P3",
                             Pw(4), "P3", Pw(5), "P3T", 6, "P4", "P4T"),
                            (Pw(7), "P4T", Pw(6), "P4",
                             Pw(6), "P4", Pw(7), "P4T", 8, "P5", "P5T"),
                        ]
                        ysrc = [(Pw(0), "P1"), (Pw(2), "P2"), (Pw(4), "P3"),
                                (Pw(6), "P4"), (Pw(8), "P5")]
                        yi = 0
                        for lev in range(5):
                            SQ2(*sq_args[lev])
                            lp, lk = ysrc[lev]
                            rT([MM(pb[3][:, 384:512], lp, Yc(yi), True, True)],
                               reads=(lk, f"Y{yi}"), writes=("pb3",))
                            rV(lambda e, yi=yi: e.tensor_tensor(
                                out=Yc(1 - yi), in0=pb[3][:, 384:512],
                                in1=Yc(yi), op=ALU.add),
                               reads=("pb3", f"Y{yi}"), writes=(f"Y{1 - yi}",))
                            yi = 1 - yi
                        rT([MM(pb[4][:, 384:512], Pw(9), Pw(8), True, True)],
                           reads=("P5", "P5T"), writes=("pb4",))
                        rS(lambda e: e.copy(out=Pw(0), in_=pb[4][:, 384:512]),
                           reads=("pb4",), writes=("P6", "P1"))
                        rT([MM(pb[3][:, 384:512], Pw(0), Yc(yi), True, True)],
                           reads=("P6", "P1", f"Y{yi}"), writes=("pb3",))
                        rV(lambda e, yi=yi: e.tensor_tensor(
                            out=Yc(1 - yi), in0=pb[3][:, 384:512],
                            in1=Yc(yi), op=ALU.add),
                           reads=("pb3", f"Y{yi}"), writes=(f"Y{1 - yi}",))
                        rV(lambda e, yi=yi, j=j: e.tensor_copy(
                            out=yapI(j), in_=Yc(1 - yi)),
                           reads=(f"Y{1 - yi}",), writes=(f"YI{j}",))
                        return r

                    def ph2_ops(h, cc=cc):
                        r = []
                        hc = h * 2 + cc
                        j = h % 2
                        def rT(ems, reads=(), writes=()):
                            r.append(("tensor", ems, reads, writes))
                        def rV(em, reads=(), writes=()):
                            r.append(("vector", [em], reads, writes))
                        def rS(em, reads=(), writes=()):
                            r.append(("scalar", [em], reads, writes))
                        rT([MM(pb[5][:, 0:256], yapI(j), Vb_(h, cc),
                               True, True),
                            MM(pb[5][:, 256:384], BKt(h, cc), yapI(j),
                               True, True)],
                           reads=(f"YI{j}", "Vb", "BKt"), writes=("pb5",))
                        rS(lambda e: e.copy(out=WT_[:], in_=pb[5][:, 256:384]),
                           reads=("pb5",), writes=("WTk",))
                        rS(lambda e: e.copy(out=VpS, in_=pb[5][:, 0:256]),
                           reads=("pb5",), writes=("Vp",))
                        rT([MM(pb[6][:, 0:256], WT_[:], Sbf(h), True, True)],
                           reads=("WTk", f"Sbf{h}"), writes=("pb6",))
                        rV(lambda e: e.scalar_tensor_tensor(
                            out=U_[:], in0=pb[6][:, 0:256], scalar=-1.0,
                            in1=VpS, op0=ALU.mult, op1=ALU.add),
                           reads=("Vp", "pb6"), writes=("U",))
                        rT([MM(pb[6][:, 256:512], QTf(h, cc), Sbf(h),
                               True, True)],
                           reads=("QTf", f"Sbf{h}"), writes=("pb6",))
                        rV(lambda e, hc=hc: e.tensor_scalar(
                            out=qsb2, in0=pb[6][:, 256:512],
                            scalar1=gamT[:, hc:hc + 1], scalar2=None,
                            op0=ALU.mult),
                           reads=("pb6", "gamT"), writes=("qsb",))
                        rT([MM(pb[6][:, 256:512], AqkI(j), U_[:], True, True)],
                           reads=(f"AqkI{j}", "U"), writes=("pb6",))
                        rV(lambda e: e.tensor_tensor(
                            out=qsb2, in0=pb[6][:, 256:512], in1=qsb2,
                            op=ALU.add),
                           reads=("qsb", "pb6"), writes=("qsb",))
                        rT([MM(pb[7][:, 256:512], Kh_(h, cc), U_[:],
                               True, True)],
                           reads=("Kh", "U"), writes=("pb7",))
                        rV(lambda e, h=h, hc=hc: e.tensor_scalar(
                            out=Sst(h), in0=Sst(h),
                            scalar1=ebC[:, hc:hc + 1], scalar2=None,
                            op0=ALU.mult),
                           reads=(f"S{h}", "ebC"), writes=(f"S{h}",))
                        rV(lambda e, h=h: e.tensor_tensor(
                            out=Sst(h), in0=pb[7][:, 256:512], in1=Sst(h),
                            op=ALU.add),
                           reads=(f"S{h}", "pb7"), writes=(f"S{h}",))
                        rS(lambda e, h=h: e.copy(out=Sbf(h), in_=Sst(h)),
                           reads=(f"S{h}",), writes=(f"Sbf{h}",))
                        rS(lambda e: e.activation(
                            out=pb[7][:, 0:256], in_=qsb2, func=AF.Square,
                            accum_out=colB[:, 48:49]),
                           reads=("qsb",), writes=("pb7", "sso"))
                        rV(lambda e: e.tensor_scalar(
                            out=colB[:, 49:50], in0=colB[:, 48:49],
                            scalar1=float(1.0 / DV), scalar2=1e-5,
                            op0=ALU.mult, op1=ALU.add),
                           reads=("sso",), writes=("ssoc",))
                        rS(lambda e: e.activation(out=colB[:, 50:51],
                                                  in_=colB[:, 49:50],
                                                  func=AF.Ln),
                           reads=("ssoc",), writes=("ssol",))
                        rS(lambda e: e.activation(out=colB[:, 51:52],
                                                  in_=colB[:, 50:51],
                                                  func=AF.Exp, scale=-0.5),
                           reads=("ssol",), writes=("rstdo",))
                        rV(lambda e: e.tensor_scalar(
                            out=qsb2, in0=qsb2, scalar1=colB[:, 51:52],
                            scalar2=None, op0=ALU.mult),
                           reads=("qsb", "rstdo"), writes=("qsb",))
                        rV(lambda e, cc=cc, h=h: e.tensor_tensor(
                            out=og2[:], in0=qsb2, in1=GVs(cc, h),
                            op=ALU.mult),
                           reads=("qsb", "GV"), writes=("U",))
                        for dv in range(2):
                            rT([MM(pb[2][:, 0:128],
                                   og2[:, dv * 128:(dv + 1) * 128],
                                   identb[:], True, True)],
                               reads=("U", "identb"), writes=("pb2",))
                            rV(lambda e: e.tensor_copy(
                                out=ogf2, in_=pb[2][:, 0:128]),
                               reads=("pb2",), writes=("ogf",))
                            ic = h * 2 + dv
                            ems = [MM(pb[hh][:, 0:512], ogf2,
                                      A[:, 49152 + ic * 1024 + hh * 512:
                                        49152 + ic * 1024 + hh * 512 + 512],
                                      ic == 0, ic == 15) for hh in range(2)]
                            rT(ems, reads=("ogf", "Wo"),
                               writes=("pb0", "pb1"))
                        return r

                    if attn_stage >= 1:
                        emit_ctx(merge2(gvE.l, ph1_ops(0)))
                        for h in range(1, H):
                            emit_ctx(merge2(ph2_ops(h - 1), ph1_ops(h)))
                        emit_ctx(ph2_ops(H - 1))
                    else:
                        emit_ctx(gvE.l)
                    for hh in range(2):
                        if attn_stage < 1:
                            break
                        V_(lambda e, hh=hh, r=rt: e.tensor_tensor(
                            out=r[:, hh * 512:(hh + 1) * 512],
                            in0=pb[hh][:, 0:512],
                            in1=r[:, hh * 512:(hh + 1) * 512],
                            op=ALU.add),
                           reads=(f"pb{hh}", rtk), writes=(rtk,))
                    ldma(res_d[gidx * 128:(gidx + 1) * 128, :], rt[:],
                         reads=(rtk,), writes=(f"res{gidx}",))

        if do_ffn1:
            ffn_sweep1(x_d, 0)
            tp.barrier()
            ffn_sweep2(x_d, res_d)
            tp.barrier()
        else:
            ldma(res_d, x_d,
                 writes=tuple(f"res{i}" for i in range(NT)))
            tp.barrier()

        if do_attn:
            build_attn()
            tp.barrier()

        if do_ffn2:
            ldma(A_gate, tern_g_d.rearrange("(a p) c -> p a c", a=8, p=128),
                 reads=("terng_dram",), writes=("terng_t",))
            ldma(A_up, tern_u_d.rearrange("(a p) c -> p a c", a=8, p=128),
                 reads=("ternu_dram",), writes=("ternu_t",))
            tp.barrier()
            ffn_sweep1(res_d, 1)
            tp.barrier()
            ffn_sweep2(res_d, out_d)
        else:
            tp.barrier()
            for i in range(NT):
                par = i % 2
                rt = rt0 if par == 0 else rt1
                ldma(rt[:], res_d[i * 128:(i + 1) * 128, :],
                     reads=(f"res{i}",), writes=(f"cp{par}",))
                ldma(out_d[i * 128:(i + 1) * 128, :], rt[:],
                     reads=(f"cp{par}",), writes=(f"cpo{par}",))

        if dbg:
            dbgA = dt("dbgA", [128, 65536], BF16, kind="ExternalOutput").ap()
            ldma(dbgA, A[:], writes=("dbgA",))
            dbgB = dt("dbgB", [128, 20480], BF16, kind="ExternalOutput").ap()
            dbgS1 = dt("dbgS1", [128, 4096], BF16, kind="ExternalOutput").ap()
            dbgS2 = dt("dbgS2", [128, 4096], BF16, kind="ExternalOutput").ap()
            tp.barrier()
            ldma(dbgB, B[:], writes=("dbgB",))
            ldma(dbgS1, S1[:], writes=("dbgS1",))
            ldma(dbgS2, S2[:], writes=("dbgS2",))

        tp.barrier()
        for li in range(NDMALANES):
            tp.wait("sync", f"s_d{li}", tp.val(f"s_d{li}"))

        block = es.enter_context(nc.Block())

        @block.sync
        def _(e):
            for f in tp.ops["sync"]:
                f(e)

        @block.tensor
        def _(e):
            for f in tp.ops["tensor"]:
                f(e)

        @block.vector
        def _(e):
            for f in tp.ops["vector"]:
                f(e)

        @block.scalar
        def _(e):
            for f in tp.ops["scalar"]:
                f(e)

        @block.gpsimd
        def _(e):
            for f in tp.ops["gpsimd"]:
                f(e)

    nc.finalize()
    return nc



def _consts():
    i = np.arange(128)
    mk = lambda keep: np.where(keep, np.float32(0.0),
                               np.float32(NEG)).astype(np.float32)
    return {
        "identb": np.eye(128, dtype=BFNP),
        "identf": np.eye(128, dtype=np.float32),
        "onesf": np.ones((1, 128), np.float32),
        "monesf": -np.ones((1, 128), np.float32),
        "onesf128": np.ones((128, 128), np.float32),
        "cvals": np.tile(np.asarray(CONSTVALS, np.float32)[None, :],
                         (128, 1)),
        "mask_ksl": mk(i[None, :] < i[:, None]),
    }


def _prep_shared(p):
    c = _consts()
    nw = np.asarray(p["L0_attn_norm_w"], np.float32).reshape(2, 128).T
    shared = {
        "wg_n": np.ascontiguousarray(p["L0_ffn_w_gate"], dtype=np.float32),
        "wu_n": np.ascontiguousarray(p["L0_ffn_w_up"], dtype=np.float32),
        "wd_n": np.ascontiguousarray(p["L0_ffn_w_down"], dtype=np.float32),
        "wg_t": np.ascontiguousarray(np.asarray(p["L0_ffn_w_gate"], np.float32).T),
        "wu_t": np.ascontiguousarray(np.asarray(p["L0_ffn_w_up"], np.float32).T),
        "wd_t": np.ascontiguousarray(np.asarray(p["L0_ffn_w_down"], np.float32).T),
        "wq": np.ascontiguousarray(p["L0_attn_Wq"], dtype=np.float32),
        "wk": np.ascontiguousarray(p["L0_attn_Wk"], dtype=np.float32),
        "wv": np.ascontiguousarray(p["L0_attn_Wv"], dtype=np.float32),
        "wgv": np.ascontiguousarray(p["L0_attn_Wg"], dtype=np.float32),
        "wo": np.ascontiguousarray(p["L0_attn_Wo"], dtype=np.float32),
        "wba": np.ascontiguousarray(np.concatenate(
            [np.asarray(p["L0_attn_Wb"], np.float32),
             np.asarray(p["L0_attn_Wa"], np.float32)], axis=1)),
        "convq": np.ascontiguousarray(p["L0_attn_conv_q"], dtype=np.float32),
        "convk": np.ascontiguousarray(p["L0_attn_conv_k"], dtype=np.float32),
        "convv": np.ascontiguousarray(p["L0_attn_conv_v"], dtype=np.float32),
        "alog": np.ascontiguousarray(
            np.asarray(p["L0_attn_A_log"], np.float32).reshape(8, 1)),
        "dtb": np.ascontiguousarray(
            np.asarray(p["L0_attn_dt_bias"], np.float32).reshape(8, 1)),
        "normw": np.ascontiguousarray(nw),
        **c,
    }
    return shared


def kernel(**inputs):
    x = np.ascontiguousarray(np.asarray(inputs["L0_attn_x"], np.float32))
    shared = _prep_shared(inputs)
    nc = build_core(T=4096)
    in_maps = [dict(shared, x=np.ascontiguousarray(x[b])) for b in range(8)]
    res = run_bass_kernel_spmd(nc, in_maps, core_ids=list(range(8)))
    return np.stack([res.results[b]["out"] for b in range(8)], axis=0)


if __name__ == "__main__":
    nc = build_core(T=4096)
    print("instructions:", sum(1 for _ in nc.all_instructions()))
